# version 05
# baseline (speedup 1.0000x reference)
import contextlib
import numpy as np
import ml_dtypes
import concourse.bass as bass
import concourse.mybir as mybir
from concourse.bass_utils import run_bass_kernel_spmd

F32 = mybir.dt.float32
BF16 = mybir.dt.bfloat16
AF = mybir.ActivationFunctionType
ALU = mybir.AluOpType
AX = mybir.AxisListType

B_, D, H = 4, 1024, 16
TPC = 4096
NTILE = 8
NCHUNK = 32
NE, INTER = 8, 4096
NG = 4
GT = TPC // NG

ENGINES = ("tensor", "vector", "scalar", "sync", "pool")
CFLAGS = {"corr", "rms", "tp", "sproj", "comb"}


class IR:

    def __init__(self):
        self.instrs = []
        self.last_write = {}
        self.last_read = {}
        self.slot_count = {}

    def add(self, eng, emit, reads=(), writes=(), slot=None, inc=16, tag=""):
        iid = len(self.instrs)
        writes = tuple(writes)
        if slot is not None:
            writes = writes + ("__slot_" + slot,)
        deps = set()
        for t in reads:
            w = self.last_write.get(t)
            if w is not None:
                deps.add(w)
        for t in writes:
            w = self.last_write.get(t)
            if w is not None:
                deps.add(w)
            for r in self.last_read.get(t, {}).values():
                deps.add(r)
        deps.discard(iid)
        rec = dict(iid=iid, eng=eng, emit=emit, slot=slot, deps=deps, tag=tag,
                   flag=False, semval=None, dmaval=None, inc=inc)
        if slot is not None:
            self.slot_count[slot] = self.slot_count.get(slot, 0) + inc
            rec["dmaval"] = self.slot_count[slot]
        self.instrs.append(rec)
        for t in reads:
            self.last_read.setdefault(t, {})[eng] = iid
        for t in writes:
            self.last_write[t] = iid
            self.last_read[t] = {}
        return iid

    def barrier(self):
        iid = len(self.instrs)
        self.instrs.append(dict(iid=iid, eng="__bar__", emit=None, slot=None,
                                deps=set(), tag="barrier", flag=False,
                                semval=None, dmaval=None))
        self.last_write = {}
        self.last_read = {}
        return iid

    def resolve(self):
        for rec in self.instrs:
            if rec["eng"] == "__bar__":
                continue
            keep = {}
            for d in rec["deps"]:
                p = self.instrs[d]
                if p["eng"] == "tensor" and rec["eng"] == "tensor" \
                        and p["slot"] is None:
                    continue
                key = ("s", p["slot"]) if p["slot"] is not None else ("c", p["eng"])
                if key not in keep or keep[key] < d:
                    keep[key] = d
            rec["deps"] = sorted(keep.values())
            for d in rec["deps"]:
                p = self.instrs[d]
                if p["slot"] is None:
                    p["flag"] = True
        lastof = {}
        run_slots = {}
        for rec in self.instrs:
            if rec["eng"] == "__bar__":
                rec["bar_last"] = dict(lastof)
                rec["bar_slots"] = dict(run_slots)
            elif rec["slot"] is not None:
                run_slots[rec["slot"]] = rec["dmaval"]
            else:
                lastof[rec["eng"]] = rec["iid"]
        for rec in self.instrs:
            if rec["eng"] == "__bar__":
                for iid in rec["bar_last"].values():
                    self.instrs[iid]["flag"] = True
        cnt = {e: 0 for e in ENGINES}
        for rec in self.instrs:
            if rec["eng"] != "__bar__" and rec["flag"] and rec["slot"] is None:
                cnt[rec["eng"]] += 1
                rec["semval"] = cnt[rec["eng"]]

    def emit_one(self, eng_name, eng, chain, slots):
        satisfied = {}

        def wait(key, sem, val):
            if val > satisfied.get(key, 0):
                eng.wait_ge(sem, val)
                satisfied[key] = val

        for rec in self.instrs:
            if rec["eng"] == "__bar__":
                for e, iid in rec["bar_last"].items():
                    if e != eng_name:
                        p = self.instrs[iid]
                        wait(("c", e), chain[e], p["semval"])
                for s, v in rec["bar_slots"].items():
                    wait(("s", s), slots[s], v)
                continue
            if rec["eng"] != eng_name:
                continue
            for d in rec["deps"]:
                p = self.instrs[d]
                if p["slot"] is not None:
                    wait(("s", p["slot"]), slots[p["slot"]], p["dmaval"])
                else:
                    wait(("c", p["eng"]), chain[p["eng"]], p["semval"])
            ins = rec["emit"](eng)
            if rec["slot"] is not None:
                if rec["inc"] == 16:
                    ins.then_inc(slots[rec["slot"]], 16)
                else:
                    ins.then_inc(slots[rec["slot"]])
            elif rec["flag"]:
                ins.then_inc(chain[eng_name], 1)


def build_program(phases=("A", "B", "C"), dbg=False):
    nc = bass.Bass("TRN2", target_bir_lowering=False, debug=False)

    dx = nc.dram_tensor("x", [D, TPC], F32, kind="ExternalInput").ap()
    dwq = nc.dram_tensor("wqT", [D, D], F32, kind="ExternalInput").ap()
    dwk = nc.dram_tensor("wkT", [D, D], F32, kind="ExternalInput").ap()
    dwv = nc.dram_tensor("wvT", [D, D], F32, kind="ExternalInput").ap()
    dwo = nc.dram_tensor("woT", [D, D], F32, kind="ExternalInput").ap()
    dwf = nc.dram_tensor("wfT", [D, H], F32, kind="ExternalInput").ap()
    dwg = nc.dram_tensor("wgT", [D, NE], F32, kind="ExternalInput").ap()
    dw1 = nc.dram_tensor("w1", [NE, D, INTER], F32, kind="ExternalInput").ap()
    dw2 = nc.dram_tensor("w2", [NE, INTER, D], F32, kind="ExternalInput").ap()
    db1 = nc.dram_tensor("b1p", [128, 256], F32, kind="ExternalInput").ap()
    db2 = nc.dram_tensor("b2", [NE, D], F32, kind="ExternalInput").ap()
    ddelta = nc.dram_tensor("delta", [1, H], F32, kind="ExternalInput").ap()
    dnw = nc.dram_tensor("nw", [1, D], F32, kind="ExternalInput").ap()
    dtriu = nc.dram_tensor("triu", [128, 128], F32, kind="ExternalInput").ap()
    dident = nc.dram_tensor("ident", [128, 128], F32, kind="ExternalInput").ap()
    dgsel = nc.dram_tensor("gsel", [NE, NE * 128], F32, kind="ExternalInput").ap()
    dmaskS = nc.dram_tensor("maskS", [128, 1], F32, kind="ExternalInput").ap()
    dmaskC = nc.dram_tensor("maskC", [128, 1], F32, kind="ExternalInput").ap()
    dout = nc.dram_tensor("out", [D, TPC], F32, kind="ExternalOutput").ap()

    kd = dict(kind="ExternalOutput") if dbg else {}
    xn_scr = nc.dram_tensor("xn_scr", [128, 8, TPC], BF16, **kd).ap()
    o_scr = nc.dram_tensor("o_scr", [NCHUNK, 128, D], F32, **kd).ap()
    qh_scr = nc.dram_tensor("qh_scr", [NCHUNK, 64, 2048], BF16, **kd).ap()
    gts_scr = nc.dram_tensor("gts_scr", [NE, TPC], F32, **kd).ap()
    g_scr = nc.dram_tensor("g_scr", [NE, NTILE, 128, 512], F32, **kd).ap()
    if dbg:
        dbg_lgs = nc.dram_tensor("dbg_lgs", [NCHUNK, 128, 8], F32, kind="ExternalOutput").ap()
        dbg_gts = nc.dram_tensor("dbg_gts", [NCHUNK, 128, 8], F32, kind="ExternalOutput").ap()
        dbg_lf = nc.dram_tensor("dbg_lf", [NCHUNK, 128, 16], F32, kind="ExternalOutput").ap()
        dbg_elb = nc.dram_tensor("dbg_elb", [NCHUNK, 128, 16], F32, kind="ExternalOutput").ap()
    cc_in = nc.dram_tensor("cc_in", [64, H * 64], F32).ap()
    cc_out = nc.dram_tensor("cc_out", [64, H * 64], F32).ap()

    dx3 = dx.rearrange("(k p) n -> p k n", p=128)
    dout3 = dout.rearrange("(k p) n -> p k n", p=128)

    ir = IR()
    stack = contextlib.ExitStack()

    def sbuf(name, shape, dtype):
        return nc.alloc_sbuf_tensor(name, shape, dtype).ap()

    triu = sbuf("sb_triu", [128, 128], F32)
    ident = sbuf("sb_ident", [128, 128], F32)
    maskS = sbuf("sb_maskS", [128, 1], F32)
    maskC = sbuf("sb_maskC", [128, 1], F32)
    ones_col = sbuf("ones_col", [128, 1], F32)
    ones_row = sbuf("ones_row", [1, 128], F32)

    ARENA = 205 * 1024
    arena = nc.alloc_sbuf_tensor("arena", [128, ARENA], mybir.dt.uint8).ap()

    class Carver:
        def __init__(self):
            self.off = 0

        def t(self, shape, dtype):
            n = int(np.prod(shape[1:])) * mybir.dt.size(dtype)
            n = (n + 255) // 256 * 256
            o = self.off
            assert o + n <= ARENA, (o, n)
            self.off += n
            ap = arena[:, o:o + n].bitcast(dtype)
            if len(shape) == 2:
                ap = ap[0:shape[0], 0:shape[1]]
            else:
                ap = ap[:, 0:shape[1] * shape[2]]
                ap = ap.rearrange("p (a b) -> p a b", a=shape[1])
                ap = ap[0:shape[0]]
            return ap

    ca = Carver()
    xf = [ca.t([128, 8 * 512], F32) for _ in range(2)]
    wland = arena[:, 0:32768].bitcast(F32)
    xnf = ca.t([128, 8 * 512], F32)
    xnb = [ca.t([128, 8 * 512], BF16) for _ in range(2)]
    wq_b = ca.t([128, 8 * 1024], BF16)
    wk_b = ca.t([128, 8 * 1024], BF16)
    wv_b = ca.t([128, 8 * 1024], BF16)
    wf_b = ca.t([128, 8 * 16], BF16)
    wg_f = ca.t([128, 8 * 8], F32)
    q_tm = ca.t([128, 1024], F32)
    k_tm = ca.t([128, 1024], F32)
    v_tm = ca.t([128, 1024], BF16)
    qd_tm = ca.t([128, 1024], F32)
    kd_tmf = ca.t([128, 1024], F32)
    kd_tmb = ca.t([128, 1024], BF16)
    qd_fm = ca.t([64, 2048], BF16)
    kd_fm = ca.t([64, 2048], BF16)
    qh_fm = [ca.t([64, 2048], BF16) for _ in range(2)]
    a_bf4 = [ca.t([128, 512], BF16) for _ in range(2)]
    sqb = [ca.t([128, 512], F32) for _ in range(2)]
    o_tm = [ca.t([128, 1024], F32) for _ in range(2)]
    lf_sb = ca.t([128, 16], F32)
    zf_sb = ca.t([128, 16], F32)
    az_sb = ca.t([128, 16], F32)
    e1f_sb = ca.t([128, 16], F32)
    l1f_sb = ca.t([128, 16], F32)
    zm_sb = ca.t([128, 16], F32)
    expLb = ca.t([128, 16], F32)
    emLb = ca.t([128, 16], F32)
    eTotD = ca.t([128, 16], F32)
    PvecD = ca.t([128, 16], F32)
    PM = ca.t([128, 16], F32)
    totR = ca.t([1, 16], F32)
    delta_b = ca.t([128, 16], F32)
    S_f = ca.t([64, 1024], F32)
    S_bf = ca.t([64, 1024], BF16)
    Scon = ca.t([64, 1024], F32)
    lgs_sb = ca.t([128, 8], F32)
    m1g = ca.t([128, 1], F32)
    m2g = ca.t([128, 1], F32)
    e1g = ca.t([128, 8], F32)
    e2g = ca.t([128, 8], F32)
    lg2 = ca.t([128, 8], F32)
    w1g = ca.t([128, 1], F32)
    w2g = ca.t([128, 1], F32)
    gts_tm = ca.t([128, 8], F32)
    gtsT_t = ca.t([8, 512], F32)
    mrow = ca.t([1, 512], F32)
    vrow = ca.t([1, 512], F32)
    m2row = ca.t([1, 512], F32)
    srow = ca.t([1, 512], F32)
    rrow = ca.t([1, 512], F32)
    brow = ca.t([1, 512], F32)
    drow = ca.t([1, 16], F32)

    cb = Carver()
    ht = cb.t([128, 32 * 1024], BF16)
    y_sb = cb.t([128, 8 * 1024], F32)
    xng = [cb.t([128, 8 * 1024], BF16) for _ in range(2)]
    w1l = [cb.t([128, 1024], F32) for _ in range(3)]
    w1b = [cb.t([128, 1024], BF16) for _ in range(3)]
    w2l = [cb.t([128, 2048], F32) for _ in range(2)]
    w2b = [cb.t([128, 4096], BF16) for _ in range(2)]
    gb_sb = [cb.t([128, 512], F32) for _ in range(2)]
    _tmpb1 = cb.t([128, 512], F32)
    tmpb = [_tmpb1, _tmpb1]
    gstage = [cb.t([128, 1024], F32) for _ in range(2)]
    gtsT_g = cb.t([8, 1024], F32)
    b1_sb = cb.t([128, 256], F32)
    b2_sb = cb.t([8, 1024], F32)
    gsel_sb = cb.t([8, 1024], F32)

    cv = Carver()
    wo_b = cv.t([128, 8 * 1024], BF16)
    wlandc = cv.t([128, 8192], F32)
    o_ld = [cv.t([128, 1024], F32) for _ in range(2)]
    qh_ld = [cv.t([64, 2048], BF16) for _ in range(2)]
    of_sb = cv.t([128, 1024], F32)
    sqc = cv.t([128, 1024], F32)
    ssum = cv.t([128, 1], F32)
    rstd = cv.t([128, 1], F32)
    on1 = cv.t([128, 1024], F32)
    on_f = cv.t([128, 1024], F32)
    on_fm = cv.t([128, 8 * 512], BF16)
    gt_ld = cv.t([128, 8 * 512], F32)
    xt_ld = cv.t([128, 8 * 512], F32)
    ostage = [cv.t([128, 8 * 512], F32) for _ in range(2)]
    tmpc = cv.t([128, 512], F32)
    S_pair = cv.t([64, 1024], F32)
    Scor_f = cv.t([64, 1024], F32)
    Scor_b = cv.t([64, 1024], BF16)
    nw_row = cv.t([1, 1024], F32)
    nw_bc = cv.t([128, 1024], F32)
    assert max(ca.off, cb.off, cv.off) <= ARENA, (ca.off, cb.off, cv.off)

    pb = [stack.enter_context(nc.psum_tensor("pb%d" % i, [128, 512], F32))
          for i in range(8)]
    pbt = ["pb%d" % i for i in range(8)]

    A = ir.add

    def mm(bank_ap, lhsT, rhs, start, stop, reads, wtok, tag=""):
        A("tensor",
          (lambda e, o=bank_ap, l=lhsT, r=rhs, st=start, sp=stop:
           e.matmul(o, l, r, start=st, stop=sp)),
          reads=reads, writes=(wtok,), tag=tag)

    A("vector", lambda e: e.memset(ones_col[:], 1.0), writes=("ones_col",))
    A("vector", lambda e: e.memset(ones_row[:], 1.0), writes=("ones_row",))
    A("vector", lambda e: e.memset(S_f[:], 0.0), writes=("S_f",))
    A("vector", lambda e: e.memset(S_bf[:], 0.0), writes=("S_bf",))
    A("vector", lambda e: e.memset(PvecD[:], 1.0), writes=("PvecD",))
    A("sync", lambda e: e.dma_start(out=triu[:], in_=dtriu), writes=("triu",), slot="m0")
    A("sync", lambda e: e.dma_start(out=ident[:], in_=dident), writes=("ident",), slot="m1")
    A("sync", lambda e: e.dma_start(out=maskS[:], in_=dmaskS), writes=("maskS",), slot="m2")
    A("sync", lambda e: e.dma_start(out=maskC[:], in_=dmaskC), writes=("maskC",), slot="m3")
    A("sync", lambda e: e.dma_start(out=drow[:], in_=ddelta), writes=("drow",), slot="m4")
    A("sync", lambda e: e.dma_start(
        out=wg_f.rearrange("p (k c) -> p k c", k=8),
        in_=dwg.rearrange("(k p) c -> p k c", p=128)), writes=("wg_f",), slot="m5")
    for wdst, wtok, src in ((wq_b, "wq_b", dwq), (wk_b, "wk_b", dwk),
                            (wv_b, "wv_b", dwv)):
        A("sync", (lambda e, s=src: e.dma_start(
            out=wland.rearrange("p (k c) -> p k c", k=8),
            in_=s.rearrange("(k p) c -> p k c", p=128))),
          writes=("wland",), slot="wld")
        A("scalar", (lambda e, d=wdst: e.copy(out=d, in_=wland[:])),
          reads=("wland",), writes=(wtok,))
    A("sync", lambda e: e.dma_start(
        out=wland.rearrange("p (k c) -> p k c", k=8)[:, :, 0:16],
        in_=dwf.rearrange("(k p) c -> p k c", p=128)),
      writes=("wland",), slot="wld")
    A("scalar", lambda e: e.copy(
        out=wf_b.rearrange("p (k c) -> p k c", k=8),
        in_=wland.rearrange("p (k c) -> p k c", k=8)[:, :, 0:16]),
      reads=("wland",), writes=("wf_b",))
    mm(pb[2][0:128, 0:16], ones_row[:], drow[:], True, True,
       ("ones_row", "drow"), pbt[2], "deltab")
    A("vector", lambda e: e.tensor_copy(out=delta_b[:], in_=pb[2][0:128, 0:16]),
      reads=(pbt[2],), writes=("delta_b",))

    ir.barrier()

    for t in range(NTILE if "A" in phases else 0):
        xb = t % 2
        xtok = "xf%d" % xb
        xntok = "xnb%d" % xb
        A("sync", (lambda e, t=t, xb=xb: e.dma_start(
            out=xf[xb].rearrange("p (k n) -> p k n", k=8),
            in_=dx3[:, :, t * 512:(t + 1) * 512])),
          writes=(xtok,), slot=xtok)
        for k in range(8):
            mm(pb[6][0:1, 0:512], ones_col[:],
               xf[xb][:, k * 512:(k + 1) * 512],
               k == 0, k == 7, ("ones_col", xtok), pbt[6], "lnsum")
        for k in range(8):
            sq = sqb[k % 2]
            sqt = "sqb%d" % (k % 2)
            A("scalar", (lambda e, k=k, xb=xb, sq=sq: e.activation(
                out=sq[:], in_=xf[xb][:, k * 512:(k + 1) * 512], func=AF.Square)),
              reads=(xtok,), writes=(sqt,))
            mm(pb[7][0:1, 0:512], ones_col[:], sq[:], k == 0, k == 7,
               ("ones_col", sqt), pbt[7], "lnsq")
        A("vector", lambda e: e.tensor_scalar_mul(mrow[:], pb[6][0:1, 0:512], 1.0 / 1024.0),
          reads=(pbt[6],), writes=("mrow",))
        A("vector", lambda e: e.tensor_scalar_mul(vrow[:], pb[7][0:1, 0:512], 1.0 / 1024.0),
          reads=(pbt[7],), writes=("vrow",))
        A("vector", lambda e: e.tensor_tensor(out=m2row[:], in0=mrow[:], in1=mrow[:], op=ALU.mult),
          reads=("mrow",), writes=("m2row",))
        A("vector", lambda e: e.tensor_sub(vrow[:], vrow[:], m2row[:]),
          reads=("vrow", "m2row"), writes=("vrow",))
        A("vector", lambda e: e.tensor_scalar_add(vrow[:], vrow[:], 1e-5),
          reads=("vrow",), writes=("vrow",))
        A("scalar", lambda e: e.activation(out=srow[:], in_=vrow[:], func=AF.Sqrt),
          reads=("vrow",), writes=("srow",))
        A("vector", lambda e: e.reciprocal(out=rrow[:], in_=srow[:]),
          reads=("srow",), writes=("rrow",))
        A("vector", lambda e: e.tensor_tensor(out=brow[:], in0=mrow[:], in1=rrow[:], op=ALU.mult),
          reads=("mrow", "rrow"), writes=("brow",))
        A("vector", lambda e: e.tensor_scalar_mul(brow[:], brow[:], -1.0),
          reads=("brow",), writes=("brow",))
        mm(pb[2][:, 0:512], ones_row[:], rrow[:], True, True,
           ("ones_row", "rrow"), pbt[2], "bca")
        mm(pb[3][:, 0:512], ones_row[:], brow[:], True, True,
           ("ones_row", "brow"), pbt[3], "bcb")
        def _b8(big, small):
            return bass.broadcast_tensor_aps(
                big.rearrange("p (u n) -> p u n", u=8),
                small.rearrange("p (u n) -> p u n", u=1))
        A("vector", (lambda e, xb=xb: e.tensor_tensor(
            out=xnf.rearrange("p (u n) -> p u n", u=8),
            in0=_b8(xf[xb], pb[2][:, 0:512])[0],
            in1=_b8(xf[xb], pb[2][:, 0:512])[1], op=ALU.mult)),
          reads=(xtok, pbt[2]), writes=("xnf",))
        A("vector", (lambda e: e.tensor_tensor(
            out=xnf.rearrange("p (u n) -> p u n", u=8),
            in0=xnf.rearrange("p (u n) -> p u n", u=8),
            in1=_b8(xnf, pb[3][:, 0:512])[1], op=ALU.add)),
          reads=("xnf", pbt[3]), writes=("xnf",))
        A("scalar", (lambda e, xb=xb: e.copy(out=xnb[xb][:], in_=xnf[:])),
          reads=("xnf",), writes=(xntok,))
        A("pool", (lambda e, t=t, xb=xb: e.dma_start(
            out=xn_scr[:, :, t * 512:(t + 1) * 512],
            in_=xnb[xb].rearrange("p (k n) -> p k n", k=8))),
          reads=(xntok,), slot="xns%d" % xb)

        for cc_i in range(4):
            c = t * 4 + cc_i

            def xsl(k, cc_i=cc_i, xb=xb):
                return xnb[xb][:, k * 512 + cc_i * 128: k * 512 + cc_i * 128 + 128]

            def xslf(k, cc_i=cc_i):
                return xnf[:, k * 512 + cc_i * 128: k * 512 + cc_i * 128 + 128]

            for k in range(8):
                mm(pb[0][:], xsl(k), wq_b[:, k * 1024:k * 1024 + 512],
                   k == 0, k == 7, (xntok, "wq_b"), pbt[0], "q0")
                mm(pb[1][:], xsl(k), wq_b[:, k * 1024 + 512:(k + 1) * 1024],
                   k == 0, k == 7, (xntok, "wq_b"), pbt[1], "q1")
            A("scalar", lambda e: e.activation(out=q_tm[:, 0:512], in_=pb[0][:], func=AF.Silu),
              reads=(pbt[0],), writes=("q_tm",))
            A("scalar", lambda e: e.activation(out=q_tm[:, 512:1024], in_=pb[1][:], func=AF.Silu),
              reads=(pbt[1],), writes=("q_tm",))
            for k in range(8):
                mm(pb[2][:], xsl(k), wk_b[:, k * 1024:k * 1024 + 512],
                   k == 0, k == 7, (xntok, "wk_b"), pbt[2], "k0")
                mm(pb[3][:], xsl(k), wk_b[:, k * 1024 + 512:(k + 1) * 1024],
                   k == 0, k == 7, (xntok, "wk_b"), pbt[3], "k1")
            A("scalar", lambda e: e.activation(out=k_tm[:, 0:512], in_=pb[2][:], func=AF.Silu),
              reads=(pbt[2],), writes=("k_tm",))
            A("scalar", lambda e: e.activation(out=k_tm[:, 512:1024], in_=pb[3][:], func=AF.Silu),
              reads=(pbt[3],), writes=("k_tm",))
            for k in range(8):
                mm(pb[0][:], xsl(k), wv_b[:, k * 1024:k * 1024 + 512],
                   k == 0, k == 7, (xntok, "wv_b"), pbt[0], "v0")
                mm(pb[1][:], xsl(k), wv_b[:, k * 1024 + 512:(k + 1) * 1024],
                   k == 0, k == 7, (xntok, "wv_b"), pbt[1], "v1")
            A("scalar", lambda e: e.activation(out=v_tm[:, 0:512], in_=pb[0][:], func=AF.Copy),
              reads=(pbt[0],), writes=("v_tm",))
            A("scalar", lambda e: e.activation(out=v_tm[:, 512:1024], in_=pb[1][:], func=AF.Copy),
              reads=(pbt[1],), writes=("v_tm",))
            for k in range(8):
                mm(pb[6][0:128, 0:16], xsl(k), wf_b[:, k * 16:(k + 1) * 16],
                   k == 0, k == 7, (xntok, "wf_b"), pbt[6], "f")
                mm(pb[7][0:128, 0:8], xslf(k), wg_f[:, k * 8:(k + 1) * 8],
                   k == 0, k == 7, ("xnf", "wg_f"), pbt[7], "lg")
            A("vector", lambda e: e.tensor_tensor(out=zf_sb[:], in0=pb[6][0:128, 0:16],
                                                  in1=delta_b[:], op=ALU.add),
              reads=(pbt[6], "delta_b"), writes=("zf",))
            A("scalar", lambda e: e.activation(out=az_sb[:], in_=zf_sb[:], func=AF.Abs),
              reads=("zf",), writes=("az",))
            A("scalar", lambda e: e.activation(out=e1f_sb[:], in_=az_sb[:], func=AF.Exp, scale=-1.0),
              reads=("az",), writes=("e1f",))
            A("scalar", lambda e: e.activation(out=l1f_sb[:], in_=e1f_sb[:], func=AF.Ln, bias=1.0),
              reads=("e1f",), writes=("l1f",))
            A("vector", lambda e: e.tensor_single_scalar(out=zm_sb[:], in_=zf_sb[:],
                                                       scalar=0.0, op=ALU.min),
              reads=("zf",), writes=("zm",))
            A("pool", lambda e: e.tensor_sub(lf_sb[:], zm_sb[:], l1f_sb[:]),
              reads=("zm", "l1f"), writes=("lf",))
            A("vector", lambda e: e.tensor_copy(out=lgs_sb[:], in_=pb[7][0:128, 0:8]),
              reads=(pbt[7],), writes=("lgs",))
            A("vector", lambda e: e.reduce_max(out=m1g[:], in_=lgs_sb[:], axis=AX.X),
              reads=("lgs",), writes=("m1g",))
            A("vector", lambda e: e.tensor_single_scalar(out=e1g[:], in_=lgs_sb[:],
                                                       scalar=m1g[:, 0:1], op=ALU.is_ge),
              reads=("lgs", "m1g"), writes=("e1g",))
            A("vector", lambda e: e.scalar_tensor_tensor(out=lg2[:], in0=e1g[:], scalar=-1e30,
                                                       in1=lgs_sb[:], op0=ALU.mult, op1=ALU.add),
              reads=("e1g", "lgs"), writes=("lg2",))
            A("vector", lambda e: e.reduce_max(out=m2g[:], in_=lg2[:], axis=AX.X),
              reads=("lg2",), writes=("m2g",))
            A("vector", lambda e: e.tensor_single_scalar(out=e2g[:], in_=lg2[:],
                                                       scalar=m2g[:, 0:1], op=ALU.is_ge),
              reads=("lg2", "m2g"), writes=("e2g",))
            A("scalar", lambda e: e.activation(out=w1g[:], in_=m2g[:], func=AF.Sigmoid,
                                               bias=m1g[:, 0:1], scale=-1.0),
              reads=("m2g", "m1g"), writes=("w1g",))
            A("vector", lambda e: e.tensor_scalar(out=w2g[:], in0=w1g[:], scalar1=-1.0,
                                                scalar2=1.0, op0=ALU.mult, op1=ALU.add),
              reads=("w1g",), writes=("w2g",))
            A("vector", lambda e: e.tensor_scalar_mul(gts_tm[:], e1g[:], w1g[:, 0:1]),
              reads=("e1g", "w1g"), writes=("gts_tm",))
            A("vector", lambda e: e.scalar_tensor_tensor(out=gts_tm[:], in0=e2g[:],
                                                       scalar=w2g[:, 0:1], in1=gts_tm[:],
                                                       op0=ALU.mult, op1=ALU.add),
              reads=("e2g", "w2g", "gts_tm"), writes=("gts_tm",))
            if dbg:
                A("pool", (lambda e, c=c: e.dma_start(out=dbg_lgs[c], in_=lgs_sb[:])),
                  reads=("lgs",), slot="dbg0")
                A("pool", (lambda e, c=c: e.dma_start(out=dbg_gts[c], in_=gts_tm[:])),
                  reads=("gts_tm",), slot="dbg1")
                A("pool", (lambda e, c=c: e.dma_start(out=dbg_lf[c], in_=lf_sb[:])),
                  reads=("lf",), slot="dbg2")
                A("pool", (lambda e, c=c: e.dma_start(out=dbg_elb[c], in_=expLb[:])),
                  reads=("expLb",), slot="dbg3")
            A("tensor", lambda e: e.transpose(pb[2][0:8, 0:128], gts_tm[:], ident[:]),
              reads=("gts_tm", "ident"), writes=(pbt[2],))
            A("vector", (lambda e, cc_i=cc_i: e.tensor_copy(
                out=gtsT_t[:, cc_i * 128:(cc_i + 1) * 128], in_=pb[2][0:8, 0:128])),
              reads=(pbt[2],), writes=("gtsT_t",))
            mm(pb[6][0:128, 0:16], triu[:], lf_sb[:], True, True,
               ("triu", "lf"), pbt[6], "cumsum")
            A("scalar", lambda e: e.activation(out=expLb[:], in_=pb[6][0:128, 0:16], func=AF.Exp),
              reads=(pbt[6],), writes=("expLb",))
            A("scalar", lambda e: e.activation(out=emLb[:], in_=pb[6][0:128, 0:16],
                                               func=AF.Exp, scale=-1.0),
              reads=(pbt[6],), writes=("emLb",))
            mm(pb[7][0:1, 0:16], ones_col[:], lf_sb[:], True, True,
               ("ones_col", "lf"), pbt[7], "tot")
            A("vector", lambda e: e.tensor_copy(out=totR[:], in_=pb[7][0:1, 0:16]),
              reads=(pbt[7],), writes=("totR",))
            mm(pb[7][0:128, 0:16], ones_row[:], totR[:], True, True,
               ("ones_row", "totR"), pbt[7], "totb")
            A("scalar", lambda e: e.activation(out=eTotD[:], in_=pb[7][0:128, 0:16], func=AF.Exp),
              reads=(pbt[7],), writes=("eTotD",))
            def _bh(big, small, hh=16):
                return bass.broadcast_tensor_aps(
                    big.rearrange("p (h n) -> p h n", h=hh),
                    small.rearrange("p (h u) -> p h u", u=1))
            A("vector", (lambda e: e.tensor_tensor(
                out=qd_tm.rearrange("p (h n) -> p h n", h=16),
                in0=_bh(q_tm, expLb)[0], in1=_bh(q_tm, expLb)[1], op=ALU.mult)),
              reads=("q_tm", "expLb"), writes=("qd_tm",))
            A("vector", (lambda e: e.tensor_tensor(
                out=kd_tmf.rearrange("p (h n) -> p h n", h=16),
                in0=_bh(k_tm, emLb)[0], in1=_bh(k_tm, emLb)[1], op=ALU.mult)),
              reads=("k_tm", "emLb"), writes=("kd_tmf",))
            A("vector", (lambda e: e.tensor_tensor(
                out=kd_tmb.rearrange("p (h n) -> p h n", h=16),
                in0=_bh(k_tm, emLb)[0], in1=_bh(k_tm, emLb)[1], op=ALU.mult)),
              reads=("k_tm", "emLb"), writes=("kd_tmb",))
            for grp in range(4):
                bank = 2 + (grp % 2)
                for j in range(4):
                    h = grp * 4 + j
                    A("tensor", (lambda e, h=h, j=j, bank=bank: e.transpose(
                        pb[bank][0:64, j * 128:(j + 1) * 128],
                        qd_tm[:, h * 64:(h + 1) * 64], ident[:])),
                      reads=("qd_tm", "ident"), writes=(pbt[bank],))
                A("scalar", (lambda e, grp=grp, bank=bank: e.copy(
                    out=qd_fm[:, grp * 512:(grp + 1) * 512], in_=pb[bank][0:64, 0:512])),
                  reads=(pbt[bank],), writes=("qd_fm",))
            for grp in range(4):
                bank = 2 + (grp % 2)
                for j in range(4):
                    h = grp * 4 + j
                    A("tensor", (lambda e, h=h, j=j, bank=bank: e.transpose(
                        pb[bank][0:64, j * 128:(j + 1) * 128],
                        kd_tmf[:, h * 64:(h + 1) * 64], ident[:])),
                      reads=("kd_tmf", "ident"), writes=(pbt[bank],))
                A("vector", (lambda e, grp=grp, bank=bank: e.tensor_copy(
                    out=kd_fm[:, grp * 512:(grp + 1) * 512], in_=pb[bank][0:64, 0:512])),
                  reads=(pbt[bank],), writes=("kd_fm",))
            qhb = c % 2
            A("vector", lambda e: e.tensor_scalar_mul(PM[:], PvecD[:], maskS[:, 0:1]),
              reads=("PvecD", "maskS"), writes=("PM",))
            A("vector", (lambda e, qhb=qhb: e.tensor_tensor(
                out=qh_fm[qhb].rearrange("p (h n) -> p h n", h=16),
                in0=_bh(qd_fm, PM[0:64, :])[0],
                in1=_bh(qd_fm, PM[0:64, :])[1], op=ALU.mult)),
              reads=("qd_fm", "PM"), writes=("qh_fm%d" % qhb,))
            A("pool", (lambda e, c=c, qhb=qhb: e.dma_start(
                out=qh_scr[c], in_=qh_fm[qhb][:])),
              reads=("qh_fm%d" % qhb,), slot="qs%d" % qhb)
            ob = c % 2
            for grp in range(4):
                abank = 2 + (grp % 2)
                aslot = grp % 2
                for j in range(4):
                    h = grp * 4 + j
                    mm(pb[abank][:, j * 128:(j + 1) * 128],
                       kd_fm[:, h * 128:(h + 1) * 128],
                       qd_fm[:, h * 128:(h + 1) * 128],
                       True, True, ("kd_fm", "qd_fm"), pbt[abank], "amm")
                A("vector", (lambda e, abank=abank, aslot=aslot: e.tensor_tensor(
                    out=a_bf4[aslot].rearrange("p (u n) -> p u n", u=4),
                    in0=pb[abank][:, 0:512].rearrange("p (u n) -> p u n", u=4),
                    in1=bass.broadcast_tensor_aps(
                        pb[abank][:, 0:512].rearrange("p (u n) -> p u n", u=4),
                        triu.rearrange("p (u n) -> p u n", u=1))[1],
                    op=ALU.mult)),
                  reads=(pbt[abank], "triu"), writes=("a_bf%d" % aslot,))
                for j in range(4):
                    h = grp * 4 + j
                    obank = 4 + (h // 8)
                    ocol = (h % 8) * 64
                    ubank = 6 + (h // 8)
                    mm(pb[obank][:, ocol:ocol + 64],
                       a_bf4[aslot][:, j * 128:(j + 1) * 128],
                       v_tm[:, h * 64:(h + 1) * 64], True, False,
                       ("a_bf%d" % aslot, "v_tm"), pbt[obank], "ointra")
                    mm(pb[obank][:, ocol:ocol + 64],
                       qd_fm[:, h * 128:(h + 1) * 128],
                       S_bf[:, h * 64:(h + 1) * 64],
                       False, True, ("qd_fm", "S_bf"), pbt[obank], "ointer")
                    mm(pb[ubank][0:64, (h % 8) * 64:(h % 8) * 64 + 64],
                       kd_tmb[:, h * 64:(h + 1) * 64],
                       v_tm[:, h * 64:(h + 1) * 64], True, True,
                       ("kd_tmb", "v_tm"), pbt[ubank], "umm")
            A("vector", lambda e: e.tensor_tensor(
                out=S_f[:, 0:512], in0=S_f[:, 0:512], in1=pb[6][0:64, 0:512],
                op=ALU.add),
              reads=("S_f", pbt[6]), writes=("S_f",))
            A("vector", lambda e: e.tensor_tensor(
                out=S_f[:, 512:1024], in0=S_f[:, 512:1024], in1=pb[7][0:64, 0:512],
                op=ALU.add),
              reads=("S_f", pbt[7]), writes=("S_f",))
            A("vector", lambda e: e.tensor_tensor(
                out=S_f.rearrange("p (h n) -> p h n", h=16),
                in0=_bh(S_f, eTotD[0:64, :])[0],
                in1=_bh(S_f, eTotD[0:64, :])[1], op=ALU.mult),
              reads=("S_f", "eTotD"), writes=("S_f",))
            A("vector", (lambda e, ob=ob: e.tensor_copy(out=o_tm[ob][:, 0:512], in_=pb[4][:])),
              reads=(pbt[4],), writes=("o_tm%d" % ob,))
            A("vector", (lambda e, ob=ob: e.tensor_copy(out=o_tm[ob][:, 512:1024], in_=pb[5][:])),
              reads=(pbt[5],), writes=("o_tm%d" % ob,))
            A("pool", (lambda e, c=c, ob=ob: e.dma_start(out=o_scr[c], in_=o_tm[ob][:])),
              reads=("o_tm%d" % ob,), slot="os%d" % ob)
            A("vector", lambda e: e.tensor_copy(out=S_bf[:], in_=S_f[:]),
              reads=("S_f",), writes=("S_bf",))
            A("vector", lambda e: e.tensor_tensor(out=PvecD[:], in0=PvecD[:],
                                                  in1=eTotD[:], op=ALU.mult),
              reads=("PvecD", "eTotD"), writes=("PvecD",))
        A("pool", (lambda e, t=t: e.dma_start(
            out=gts_scr[:, t * 512:(t + 1) * 512], in_=gtsT_t[:])),
          reads=("gtsT_t",), slot="gts")

    if "A" not in phases:
        phases = phases
    A("vector", lambda e: e.tensor_scalar_mul(Scon[:], S_f[:], maskC[0:64, 0:1]),
      reads=("S_f", "maskC"), writes=("Scon",))
    A("pool", lambda e: e.dma_start(out=cc_in, in_=Scon[:]),
      reads=("Scon",), writes=("cc_in",), slot="ccin")
    A("pool", lambda e: e.collective_compute(
        "AllReduce", ALU.add, replica_groups=[[0, 1], [2, 3], [4, 5], [6, 7]],
        ins=[cc_in.opt()], outs=[cc_out.opt()]),
      reads=("cc_in",), writes=("cc_out",), slot="cc", inc=1)

    ir.barrier()

    A("sync", lambda e: e.dma_start(out=b1_sb[:], in_=db1), writes=("b1_sb",), slot="m0")
    A("sync", lambda e: e.dma_start(out=b2_sb[:], in_=db2), writes=("b2_sb",), slot="m1")
    A("sync", lambda e: e.dma_start(out=gsel_sb[:], in_=dgsel), writes=("gsel_sb",), slot="m2")

    for g in range(NG if "B" in phases else 0):
        gxb = g % 2
        A("sync", (lambda e, g=g, gxb=gxb: e.dma_start(
            out=xng[gxb].rearrange("p (k n) -> p k n", k=8),
            in_=xn_scr[:, :, g * GT:(g + 1) * GT])),
          writes=("xng%d" % gxb,), slot="xng%d" % gxb)
        A("sync", (lambda e, g=g: e.dma_start(
            out=gtsT_g[:], in_=gts_scr[:, g * GT:(g + 1) * GT])),
          writes=("gtsT_g",), slot="gtg")
        for tt in range(2):
            for hc in range(8):
                bank = 4 + (2 * tt + hc) % 4
                mm(pb[bank][:], b2_sb[:, hc * 128:(hc + 1) * 128],
                   gtsT_g[:, tt * 512:(tt + 1) * 512], True, True,
                   ("b2_sb", "gtsT_g"), pbt[bank], "yinit")
                A("vector", (lambda e, tt=tt, hc=hc, bank=bank: e.tensor_copy(
                    out=y_sb[:, hc * 1024 + tt * 512: hc * 1024 + tt * 512 + 512],
                    in_=pb[bank][:])),
                  reads=(pbt[bank],), writes=("y_sb",))
        for e_i in range(NE):
            for m in range(32):
                ws = m % 3
                A("sync", (lambda e, e_i=e_i, m=m, ws=ws: e.dma_start(
                    out=w1l[ws].rearrange("p (k c) -> p k c", k=8),
                    in_=dw1[e_i].rearrange("(k p) c -> p k c", p=128)
                    [:, :, m * 128:(m + 1) * 128])),
                  writes=("w1l%d" % ws,), slot="w1l%d" % ws)
                A("vector", (lambda e, ws=ws: e.tensor_copy(out=w1b[ws], in_=w1l[ws][:])),
                  reads=("w1l%d" % ws,), writes=("w1b%d" % ws,))
                bA = (m % 2) * 2
                bB = bA + 1
                for k in range(8):
                    mm(pb[bA][:], w1b[ws][:, k * 128:(k + 1) * 128],
                       xng[gxb][:, k * 1024:k * 1024 + 512], k == 0, k == 7,
                       ("w1b%d" % ws, "xng%d" % gxb), pbt[bA], "h0")
                    mm(pb[bB][:], w1b[ws][:, k * 128:(k + 1) * 128],
                       xng[gxb][:, k * 1024 + 512:(k + 1) * 1024], k == 0, k == 7,
                       ("w1b%d" % ws, "xng%d" % gxb), pbt[bB], "h1")
                bias = b1_sb[:, e_i * 32 + m: e_i * 32 + m + 1]
                A("scalar", (lambda e, m=m, bA=bA, bias=bias: e.activation(
                    out=ht[:, m * 1024:m * 1024 + 512], in_=pb[bA][:],
                    func=AF.Gelu, bias=bias)),
                  reads=(pbt[bA], "b1_sb"), writes=("ht",))
                A("scalar", (lambda e, m=m, bB=bB, bias=bias: e.activation(
                    out=ht[:, m * 1024 + 512:(m + 1) * 1024], in_=pb[bB][:],
                    func=AF.Gelu, bias=bias)),
                  reads=(pbt[bB], "b1_sb"), writes=("ht",))
            for tt in range(2):
                mm(pb[tt][:], gsel_sb[:, e_i * 128:(e_i + 1) * 128],
                   gtsT_g[:, tt * 512:(tt + 1) * 512], True, True,
                   ("gsel_sb", "gtsT_g"), pbt[tt], "gbmm")
                A("vector", (lambda e, tt=tt: e.tensor_copy(out=gb_sb[tt][:], in_=pb[tt][:])),
                  reads=(pbt[tt],), writes=("gb%d" % tt,))
            for hc in range(8):
                w2s = hc % 2
                for hf in range(2):
                    A("sync", (lambda e, e_i=e_i, hc=hc, hf=hf, w2s=w2s: e.dma_start(
                        out=w2l[w2s].rearrange("p (k c) -> p k c", k=16),
                        in_=dw2[e_i].rearrange("(m p) c -> p m c", p=128)
                        [:, hf * 16:(hf + 1) * 16, hc * 128:(hc + 1) * 128])),
                      writes=("w2l%d" % w2s,), slot="w2l%d" % w2s)
                    A("scalar", (lambda e, hf=hf, w2s=w2s: e.copy(
                        out=w2b[w2s][:, hf * 2048:(hf + 1) * 2048], in_=w2l[w2s][:])),
                      reads=("w2l%d" % w2s,), writes=("w2b%d" % w2s,))
                bP = 4 + (hc % 2) * 2
                bQ = bP + 1
                for m in range(32):
                    mm(pb[bP][:], w2b[w2s][:, m * 128:(m + 1) * 128],
                       ht[:, m * 1024:m * 1024 + 512], m == 0, m == 31,
                       ("w2b%d" % w2s, "ht"), pbt[bP], "o0")
                    mm(pb[bQ][:], w2b[w2s][:, m * 128:(m + 1) * 128],
                       ht[:, m * 1024 + 512:(m + 1) * 1024], m == 0, m == 31,
                       ("w2b%d" % w2s, "ht"), pbt[bQ], "o1")
                for tt, bx in ((0, bP), (1, bQ)):
                    A("vector", (lambda e, bx=bx, tt=tt: e.tensor_tensor(
                        out=tmpb[tt][:], in0=pb[bx][:], in1=gb_sb[tt][:], op=ALU.mult)),
                      reads=(pbt[bx], "gb%d" % tt), writes=("tmpb0",))
                    ys = y_sb[:, hc * 1024 + tt * 512: hc * 1024 + tt * 512 + 512]
                    A("vector", (lambda e, ys=ys, tt=tt: e.tensor_tensor(
                        out=ys, in0=ys, in1=tmpb[tt][:], op=ALU.add)),
                      reads=("tmpb0", "y_sb"), writes=("y_sb",))
        for hc in range(8):
            gsl = hc % 2
            A("scalar", (lambda e, hc=hc, gsl=gsl: e.activation(
                out=gstage[gsl][:], in_=y_sb[:, hc * 1024:(hc + 1) * 1024], func=AF.Silu)),
              reads=("y_sb",), writes=("gstage%d" % gsl,))
            A("pool", (lambda e, g=g, hc=hc, gsl=gsl: e.dma_start(
                out=g_scr[hc, 2 * g:2 * g + 2].rearrange("a p n -> p a n"),
                in_=gstage[gsl].rearrange("p (a n) -> p a n", a=2))),
              reads=("gstage%d" % gsl,), slot="gs%d" % gsl)

    ir.barrier()

    A("sync", lambda e: e.dma_start(
        out=wlandc.rearrange("p (k c) -> p k c", k=8),
        in_=dwo.rearrange("(k p) c -> p k c", p=128)),
      writes=("wlandc",), slot="wld")
    A("scalar", lambda e: e.copy(out=wo_b, in_=wlandc[:]),
      reads=("wlandc",), writes=("wo_b",))
    A("sync", lambda e: e.dma_start(out=S_pair[:], in_=cc_out),
      reads=("cc_out",), writes=("S_pair",), slot="sld")
    A("vector", lambda e: e.tensor_scalar_mul(Scor_f[:], S_pair[:], maskS[0:64, 0:1]),
      reads=("S_pair", "maskS"), writes=("Scor_f",))
    A("vector", lambda e: e.tensor_copy(out=Scor_b[:], in_=Scor_f[:]),
      reads=("Scor_f",), writes=("Scor_b",))
    A("sync", lambda e: e.dma_start(out=nw_row[:], in_=dnw),
      writes=("nw_row",), slot="m3")
    mm(pb[2][:], ones_row[:], nw_row[:, 0:512], True, True,
       ("ones_row", "nw_row"), pbt[2], "nw0")
    mm(pb[3][:], ones_row[:], nw_row[:, 512:1024], True, True,
       ("ones_row", "nw_row"), pbt[3], "nw1")
    A("vector", lambda e: e.tensor_copy(out=nw_bc[:, 0:512], in_=pb[2][:]),
      reads=(pbt[2],), writes=("nw_bc",))
    A("vector", lambda e: e.tensor_copy(out=nw_bc[:, 512:1024], in_=pb[3][:]),
      reads=(pbt[3],), writes=("nw_bc",))

    for t in range(NTILE if "C" in phases else 0):
        A("sync", (lambda e, t=t: e.dma_start(
            out=gt_ld.rearrange("p (k n) -> p k n", k=8),
            in_=g_scr[:, t].rearrange("h p n -> p h n"))),
          writes=("gt_ld",), slot="gld")
        A("sync", (lambda e, t=t: e.dma_start(
            out=xt_ld.rearrange("p (k n) -> p k n", k=8),
            in_=dx3[:, :, t * 512:(t + 1) * 512])),
          writes=("xt_ld",), slot="xtl")
        for cc_i in range(4):
            c = t * 4 + cc_i
            lb = c % 2
            A("sync", (lambda e, c=c, lb=lb: e.dma_start(out=o_ld[lb][:], in_=o_scr[c])),
              writes=("o_ld%d" % lb,), slot="old%d" % lb)
            A("sync", (lambda e, c=c, lb=lb: e.dma_start(out=qh_ld[lb][:], in_=qh_scr[c])),
              writes=("qh_ld%d" % lb,), slot="qld%d" % lb)
            for h in range(16 if "corr" in CFLAGS else 0):
                obank = h // 8
                ocol = (h % 8) * 64
                mm(pb[obank][:, ocol:ocol + 64],
                   qh_ld[lb][:, h * 128:(h + 1) * 128],
                   Scor_b[:, h * 64:(h + 1) * 64],
                   True, True, ("qh_ld%d" % lb, "Scor_b"), pbt[obank], "corr")
            if "corr" in CFLAGS:
                A("vector", (lambda e, lb=lb: e.tensor_tensor(
                    out=of_sb[:, 0:512], in0=o_ld[lb][:, 0:512], in1=pb[0][:, 0:512],
                    op=ALU.add)),
                  reads=("o_ld%d" % lb, pbt[0]), writes=("of",))
                A("vector", (lambda e, lb=lb: e.tensor_tensor(
                    out=of_sb[:, 512:1024], in0=o_ld[lb][:, 512:1024], in1=pb[1][:, 0:512],
                    op=ALU.add)),
                  reads=("o_ld%d" % lb, pbt[1]), writes=("of",))
            else:
                A("vector", (lambda e, lb=lb: e.tensor_copy(out=of_sb[:, 0:1024],
                                                            in_=o_ld[lb][:, 0:1024])),
                  reads=("o_ld%d" % lb,), writes=("of",))
            if "rms" not in CFLAGS:
                A("vector", lambda e: e.tensor_copy(out=on_f[:], in_=of_sb[:]),
                  reads=("of",), writes=("on_f",))
            if "rms" not in CFLAGS:
                pass
            _n_rms = 1 if "rms" in CFLAGS else 0
            for _ in range(_n_rms):
                A("scalar", lambda e: e.activation(out=sqc[:], in_=of_sb[:], func=AF.Square),
                  reads=("of",), writes=("sqc",))
            for _ in range(_n_rms):
                A("vector", lambda e: e.reduce_sum(out=ssum[:], in_=sqc[:], axis=AX.X),
                  reads=("sqc",), writes=("ssum",))
            for _ in range(_n_rms):
                A("vector", lambda e: e.tensor_scalar(out=rstd[:], in0=ssum[:], scalar1=1.0 / 1024.0,
                                                      scalar2=1e-6, op0=ALU.mult, op1=ALU.add),
                  reads=("ssum",), writes=("rstd",))
            for _ in range(_n_rms):
                A("scalar", lambda e: e.activation(out=rstd[:], in_=rstd[:], func=AF.Sqrt),
                  reads=("rstd",), writes=("rstd",))
            for _ in range(_n_rms):
                A("vector", lambda e: e.reciprocal(out=rstd[:], in_=rstd[:]),
                  reads=("rstd",), writes=("rstd",))
            for _ in range(_n_rms):
                A("vector", lambda e: e.tensor_scalar_mul(on1[:], of_sb[:], rstd[:, 0:1]),
                  reads=("of", "rstd"), writes=("on1",))
            for _ in range(_n_rms):
                A("vector", lambda e: e.tensor_tensor(out=on_f[:], in0=on1[:], in1=nw_bc[:],
                                                      op=ALU.mult),
                  reads=("on1", "nw_bc"), writes=("on_f",))
            for blk in range(8):
                bank = 4 + (blk % 2)
                A("tensor", (lambda e, blk=blk, bank=bank: e.transpose(
                    pb[bank][:, 0:128], on_f[:, blk * 128:(blk + 1) * 128], ident[:])),
                  reads=("on_f", "ident"), writes=(pbt[bank],))
                A("vector", (lambda e, blk=blk, bank=bank, cc_i=cc_i: e.tensor_copy(
                    out=on_fm[:, blk * 512 + cc_i * 128: blk * 512 + cc_i * 128 + 128],
                    in_=pb[bank][:, 0:128])),
                  reads=(pbt[bank],), writes=("on_fm",))
        osb = t % 2
        for fc in range(8):
            bank = 2 + (fc % 2)
            for k in range(8):
                mm(pb[bank][:], wo_b[:, k * 1024 + fc * 128: k * 1024 + fc * 128 + 128],
                   on_fm[:, k * 512:(k + 1) * 512], k == 0, k == 7,
                   ("wo_b", "on_fm"), pbt[bank], "sproj")
            A("vector", (lambda e, fc=fc, bank=bank: e.tensor_tensor(
                out=tmpc[:], in0=pb[bank][:], in1=gt_ld[:, fc * 512:(fc + 1) * 512],
                op=ALU.mult)),
              reads=(pbt[bank], "gt_ld"), writes=("tmpc",))
            A("vector", (lambda e, fc=fc, osb=osb: e.tensor_tensor(
                out=ostage[osb][:, fc * 512:(fc + 1) * 512], in0=tmpc[:],
                in1=xt_ld[:, fc * 512:(fc + 1) * 512], op=ALU.add)),
              reads=("tmpc", "xt_ld"), writes=("ostage%d" % osb,))
        A("pool", (lambda e, t=t, osb=osb: e.dma_start(
            out=dout3[:, :, t * 512:(t + 1) * 512],
            in_=ostage[osb].rearrange("p (k n) -> p k n", k=8))),
          reads=("ostage%d" % osb,), slot="outs%d" % osb)

    ir.barrier()

    ir.resolve()
    chain, slots = {}, {}
    with contextlib.ExitStack() as semstack:
        for e in ENGINES:
            chain[e] = semstack.enter_context(nc.semaphore("c_" + e))
        for s in sorted(ir.slot_count):
            slots[s] = semstack.enter_context(nc.semaphore("s_" + s))
        with nc.Block() as block:
            @block.tensor
            def _(eng):
                ir.emit_one("tensor", eng, chain, slots)

            @block.vector
            def _(eng):
                ir.emit_one("vector", eng, chain, slots)

            @block.scalar
            def _(eng):
                ir.emit_one("scalar", eng, chain, slots)

            @block.sync
            def _(eng):
                ir.emit_one("sync", eng, chain, slots)

            @block.gpsimd
            def _(eng):
                ir.emit_one("pool", eng, chain, slots)
    stack.close()
    return nc


def kernel(L0_attn_x, L0_attn_Wq, L0_attn_Wk, L0_attn_Wv, L0_attn_Wo,
           L0_attn_Wf, L0_attn_delta, L0_attn_norm_weight, L0_ffn_Wg,
           L0_ffn_W1, L0_ffn_b1, L0_ffn_W2, L0_ffn_b2):
    f32 = np.float32
    x = np.asarray(L0_attn_x, f32)
    wqT = np.ascontiguousarray(np.asarray(L0_attn_Wq, f32).T)
    wkT = np.ascontiguousarray(np.asarray(L0_attn_Wk, f32).T)
    wvT = np.ascontiguousarray(np.asarray(L0_attn_Wv, f32).T)
    woT = np.ascontiguousarray(np.asarray(L0_attn_Wo, f32).T)
    wfT = np.ascontiguousarray(np.asarray(L0_attn_Wf, f32).T)
    wgT = np.ascontiguousarray(np.asarray(L0_ffn_Wg, f32).T)
    w1 = np.ascontiguousarray(np.asarray(L0_ffn_W1, f32))
    w2 = np.ascontiguousarray(np.asarray(L0_ffn_W2, f32))
    b1p = np.ascontiguousarray(
        np.asarray(L0_ffn_b1, f32).reshape(8, 32, 128).transpose(2, 0, 1).reshape(128, 256))
    b2 = np.ascontiguousarray(np.asarray(L0_ffn_b2, f32))
    delta = np.ascontiguousarray(np.asarray(L0_attn_delta, f32).reshape(1, 16))
    nw = np.ascontiguousarray(np.asarray(L0_attn_norm_weight, f32).reshape(1, 1024))
    triu = np.ascontiguousarray(np.triu(np.ones((128, 128), f32)))
    ident = np.eye(128, dtype=f32)
    gsel = np.zeros((8, 1024), f32)
    for e in range(8):
        gsel[e, e * 128:(e + 1) * 128] = 1.0

    nc = build_program()
    in_maps = []
    for c in range(8):
        b, half = c // 2, c % 2
        xc = np.ascontiguousarray(x[b, half * 4096:(half + 1) * 4096, :].T)
        mS = np.full((128, 1), 1.0 if half == 1 else 0.0, f32)
        mC = np.full((128, 1), 1.0 if half == 0 else 0.0, f32)
        in_maps.append(dict(
            x=xc, wqT=wqT, wkT=wkT, wvT=wvT, woT=woT, wfT=wfT, wgT=wgT,
            w1=w1, w2=w2, b1p=b1p, b2=b2, delta=delta, nw=nw, triu=triu,
            ident=ident, gsel=gsel, maskS=mS, maskC=mC))
    res = run_bass_kernel_spmd(nc, in_maps, core_ids=list(range(8)))
    out = np.empty((4, 8192, 1024), f32)
    for c in range(8):
        b, half = c // 2, c % 2
        out[b, half * 4096:(half + 1) * 4096, :] = res.results[c]["out"].T
    return out


# version 06
# speedup vs baseline: 1.1437x; 1.1437x over previous
import contextlib
import numpy as np
import ml_dtypes
import concourse.bass as bass
import concourse.mybir as mybir
from concourse.bass_utils import run_bass_kernel_spmd

F32 = mybir.dt.float32
BF16 = mybir.dt.bfloat16
AF = mybir.ActivationFunctionType
ALU = mybir.AluOpType
AX = mybir.AxisListType

B_, D, H = 4, 1024, 16
TPC = 4096
NTILE = 8
NCHUNK = 32
NE, INTER = 8, 4096
NG = 4
GT = TPC // NG

ENGINES = ("tensor", "vector", "scalar", "sync", "pool")
CFLAGS = {"corr", "rms", "tp", "sproj", "comb"}


class IR:

    def __init__(self):
        self.instrs = []
        self.last_write = {}
        self.last_read = {}
        self.slot_count = {}

    def add(self, eng, emit, reads=(), writes=(), slot=None, inc=16, tag=""):
        iid = len(self.instrs)
        writes = tuple(writes)
        if slot is not None:
            writes = writes + ("__slot_" + slot,)
        deps = set()
        for t in reads:
            w = self.last_write.get(t)
            if w is not None:
                deps.add(w)
        for t in writes:
            w = self.last_write.get(t)
            if w is not None:
                deps.add(w)
            for r in self.last_read.get(t, {}).values():
                deps.add(r)
        deps.discard(iid)
        rec = dict(iid=iid, eng=eng, emit=emit, slot=slot, deps=deps, tag=tag,
                   flag=False, semval=None, dmaval=None, inc=inc)
        if slot is not None:
            self.slot_count[slot] = self.slot_count.get(slot, 0) + inc
            rec["dmaval"] = self.slot_count[slot]
        self.instrs.append(rec)
        for t in reads:
            self.last_read.setdefault(t, {})[eng] = iid
        for t in writes:
            self.last_write[t] = iid
            self.last_read[t] = {}
        return iid

    def barrier(self):
        iid = len(self.instrs)
        self.instrs.append(dict(iid=iid, eng="__bar__", emit=None, slot=None,
                                deps=set(), tag="barrier", flag=False,
                                semval=None, dmaval=None))
        self.last_write = {}
        self.last_read = {}
        return iid

    def resolve(self):
        for rec in self.instrs:
            if rec["eng"] == "__bar__":
                continue
            keep = {}
            for d in rec["deps"]:
                p = self.instrs[d]
                if p["eng"] == "tensor" and rec["eng"] == "tensor" \
                        and p["slot"] is None:
                    continue
                key = ("s", p["slot"]) if p["slot"] is not None else ("c", p["eng"])
                if key not in keep or keep[key] < d:
                    keep[key] = d
            rec["deps"] = sorted(keep.values())
            for d in rec["deps"]:
                p = self.instrs[d]
                if p["slot"] is None:
                    p["flag"] = True
        lastof = {}
        run_slots = {}
        for rec in self.instrs:
            if rec["eng"] == "__bar__":
                rec["bar_last"] = dict(lastof)
                rec["bar_slots"] = dict(run_slots)
            elif rec["slot"] is not None:
                run_slots[rec["slot"]] = rec["dmaval"]
            else:
                lastof[rec["eng"]] = rec["iid"]
        for rec in self.instrs:
            if rec["eng"] == "__bar__":
                for iid in rec["bar_last"].values():
                    self.instrs[iid]["flag"] = True
        cnt = {e: 0 for e in ENGINES}
        for rec in self.instrs:
            if rec["eng"] != "__bar__" and rec["flag"] and rec["slot"] is None:
                cnt[rec["eng"]] += 1
                rec["semval"] = cnt[rec["eng"]]

    def emit_one(self, eng_name, eng, chain, slots):
        satisfied = {}

        def wait(key, sem, val):
            if val > satisfied.get(key, 0):
                eng.wait_ge(sem, val)
                satisfied[key] = val

        for rec in self.instrs:
            if rec["eng"] == "__bar__":
                for e, iid in rec["bar_last"].items():
                    if e != eng_name:
                        p = self.instrs[iid]
                        wait(("c", e), chain[e], p["semval"])
                for s, v in rec["bar_slots"].items():
                    wait(("s", s), slots[s], v)
                continue
            if rec["eng"] != eng_name:
                continue
            for d in rec["deps"]:
                p = self.instrs[d]
                if p["slot"] is not None:
                    wait(("s", p["slot"]), slots[p["slot"]], p["dmaval"])
                else:
                    wait(("c", p["eng"]), chain[p["eng"]], p["semval"])
            ins = rec["emit"](eng)
            if rec["slot"] is not None:
                if rec["inc"] == 16:
                    ins.then_inc(slots[rec["slot"]], 16)
                else:
                    ins.then_inc(slots[rec["slot"]])
            elif rec["flag"]:
                ins.then_inc(chain[eng_name], 1)


def build_program(phases=("A", "B", "C"), dbg=False):
    nc = bass.Bass("TRN2", target_bir_lowering=False, debug=False)

    dx = nc.dram_tensor("x", [D, TPC], F32, kind="ExternalInput").ap()
    dwq = nc.dram_tensor("wqT", [D, D], F32, kind="ExternalInput").ap()
    dwk = nc.dram_tensor("wkT", [D, D], F32, kind="ExternalInput").ap()
    dwv = nc.dram_tensor("wvT", [D, D], F32, kind="ExternalInput").ap()
    dwo = nc.dram_tensor("woT", [D, D], F32, kind="ExternalInput").ap()
    dwf = nc.dram_tensor("wfT", [D, H], F32, kind="ExternalInput").ap()
    dwg = nc.dram_tensor("wgT", [D, NE], F32, kind="ExternalInput").ap()
    dw1 = nc.dram_tensor("w1", [NE, D, INTER], F32, kind="ExternalInput").ap()
    dw2 = nc.dram_tensor("w2", [NE, INTER, D], F32, kind="ExternalInput").ap()
    db1 = nc.dram_tensor("b1p", [128, 256], F32, kind="ExternalInput").ap()
    db2 = nc.dram_tensor("b2", [NE, D], F32, kind="ExternalInput").ap()
    ddelta = nc.dram_tensor("delta", [1, H], F32, kind="ExternalInput").ap()
    dnw = nc.dram_tensor("nw", [1, D], F32, kind="ExternalInput").ap()
    dtriu = nc.dram_tensor("triu", [128, 128], F32, kind="ExternalInput").ap()
    dident = nc.dram_tensor("ident", [128, 128], F32, kind="ExternalInput").ap()
    dgsel = nc.dram_tensor("gsel", [NE, NE * 128], F32, kind="ExternalInput").ap()
    dmaskS = nc.dram_tensor("maskS", [128, 1], F32, kind="ExternalInput").ap()
    dmaskC = nc.dram_tensor("maskC", [128, 1], F32, kind="ExternalInput").ap()
    dout = nc.dram_tensor("out", [D, TPC], F32, kind="ExternalOutput").ap()

    kd = dict(kind="ExternalOutput") if dbg else {}
    xn_scr = nc.dram_tensor("xn_scr", [128, 8, TPC], BF16, **kd).ap()
    o_scr = nc.dram_tensor("o_scr", [NCHUNK, 128, D], F32, **kd).ap()
    qh_scr = nc.dram_tensor("qh_scr", [NCHUNK, 64, 2048], BF16, **kd).ap()
    gts_scr = nc.dram_tensor("gts_scr", [NE, TPC], F32, **kd).ap()
    g_scr = nc.dram_tensor("g_scr", [NE, NTILE, 128, 512], F32, **kd).ap()
    if dbg:
        dbg_lgs = nc.dram_tensor("dbg_lgs", [NCHUNK, 128, 8], F32, kind="ExternalOutput").ap()
        dbg_gts = nc.dram_tensor("dbg_gts", [NCHUNK, 128, 8], F32, kind="ExternalOutput").ap()
        dbg_lf = nc.dram_tensor("dbg_lf", [NCHUNK, 128, 16], F32, kind="ExternalOutput").ap()
        dbg_elb = nc.dram_tensor("dbg_elb", [NCHUNK, 128, 16], F32, kind="ExternalOutput").ap()
    cc_in = nc.dram_tensor("cc_in", [64, H * 64], F32).ap()
    cc_out = nc.dram_tensor("cc_out", [64, H * 64], F32).ap()

    dx3 = dx.rearrange("(k p) n -> p k n", p=128)
    dout3 = dout.rearrange("(k p) n -> p k n", p=128)

    ir = IR()
    stack = contextlib.ExitStack()

    def sbuf(name, shape, dtype):
        return nc.alloc_sbuf_tensor(name, shape, dtype).ap()

    triu = sbuf("sb_triu", [128, 128], F32)
    ident = sbuf("sb_ident", [128, 128], F32)
    maskS = sbuf("sb_maskS", [128, 1], F32)
    maskC = sbuf("sb_maskC", [128, 1], F32)
    ones_col = sbuf("ones_col", [128, 1], F32)
    ones_row = sbuf("ones_row", [1, 128], F32)

    ARENA = 205 * 1024
    arena = nc.alloc_sbuf_tensor("arena", [128, ARENA], mybir.dt.uint8).ap()

    class Carver:
        def __init__(self):
            self.off = 0

        def t(self, shape, dtype):
            n = int(np.prod(shape[1:])) * mybir.dt.size(dtype)
            n = (n + 255) // 256 * 256
            o = self.off
            assert o + n <= ARENA, (o, n)
            self.off += n
            ap = arena[:, o:o + n].bitcast(dtype)
            if len(shape) == 2:
                ap = ap[0:shape[0], 0:shape[1]]
            else:
                ap = ap[:, 0:shape[1] * shape[2]]
                ap = ap.rearrange("p (a b) -> p a b", a=shape[1])
                ap = ap[0:shape[0]]
            return ap

    ca = Carver()
    xf = [ca.t([128, 8 * 512], F32) for _ in range(2)]
    wland = arena[:, 0:32768].bitcast(F32)
    xnf = ca.t([128, 8 * 512], F32)
    xnb = [ca.t([128, 8 * 512], BF16) for _ in range(2)]
    wq_b = ca.t([128, 8 * 1024], BF16)
    wk_b = ca.t([128, 8 * 1024], BF16)
    wv_b = ca.t([128, 8 * 1024], BF16)
    wf_b = ca.t([128, 8 * 16], BF16)
    wg_f = ca.t([128, 8 * 8], F32)
    q_tm = ca.t([128, 1024], F32)
    k_tm = ca.t([128, 1024], F32)
    v_tm = ca.t([128, 1024], BF16)
    qd_tm = ca.t([128, 1024], F32)
    kd_tmf = ca.t([128, 1024], F32)
    kd_tmb = ca.t([128, 1024], BF16)
    qd_fm = ca.t([64, 2048], BF16)
    kd_fm = ca.t([64, 2048], BF16)
    qh_fm = [ca.t([64, 2048], BF16) for _ in range(2)]
    a_bf4 = [ca.t([128, 512], BF16) for _ in range(2)]
    sqb = [ca.t([128, 512], F32) for _ in range(2)]
    o_tm = [ca.t([128, 1024], F32) for _ in range(2)]
    lf_sb = ca.t([128, 16], F32)
    zf_sb = ca.t([128, 16], F32)
    az_sb = ca.t([128, 16], F32)
    e1f_sb = ca.t([128, 16], F32)
    l1f_sb = ca.t([128, 16], F32)
    zm_sb = ca.t([128, 16], F32)
    expLb = ca.t([128, 16], F32)
    emLb = ca.t([128, 16], F32)
    eTotD = ca.t([128, 16], F32)
    PvecD = ca.t([128, 16], F32)
    PM = ca.t([128, 16], F32)
    totR = ca.t([1, 16], F32)
    delta_b = ca.t([128, 16], F32)
    S_f = ca.t([64, 1024], F32)
    S_bf = ca.t([64, 1024], BF16)
    Scon = ca.t([64, 1024], F32)
    lgs_sb = ca.t([128, 8], F32)
    m1g = ca.t([128, 1], F32)
    m2g = ca.t([128, 1], F32)
    e1g = ca.t([128, 8], F32)
    e2g = ca.t([128, 8], F32)
    lg2 = ca.t([128, 8], F32)
    w1g = ca.t([128, 1], F32)
    w2g = ca.t([128, 1], F32)
    gts_tm = ca.t([128, 8], F32)
    gtsT_t = ca.t([8, 512], F32)
    mrow = ca.t([1, 512], F32)
    vrow = ca.t([1, 512], F32)
    m2row = ca.t([1, 512], F32)
    srow = ca.t([1, 512], F32)
    rrow = ca.t([1, 512], F32)
    brow = ca.t([1, 512], F32)
    drow = ca.t([1, 16], F32)

    cb = Carver()
    ht = cb.t([128, 32 * 1024], BF16)
    y_sb = cb.t([128, 8 * 1024], F32)
    xng = [cb.t([128, 8 * 1024], BF16) for _ in range(2)]
    w1l = [cb.t([128, 1024], F32) for _ in range(3)]
    w1b = [cb.t([128, 1024], BF16) for _ in range(3)]
    w2l = [cb.t([128, 2048], F32) for _ in range(2)]
    w2b = [cb.t([128, 4096], BF16) for _ in range(2)]
    gb_sb = [cb.t([128, 512], F32) for _ in range(2)]
    _tmpb1 = cb.t([128, 512], F32)
    tmpb = [_tmpb1, _tmpb1]
    gstage = [cb.t([128, 1024], F32) for _ in range(2)]
    gtsT_g = cb.t([8, 1024], F32)
    b1_sb = cb.t([128, 256], F32)
    b2_sb = cb.t([8, 1024], F32)
    gsel_sb = cb.t([8, 1024], F32)

    cv = Carver()
    wo_b = cv.t([128, 8 * 1024], BF16)
    wlandc = cv.t([128, 8192], F32)
    o_ld = [cv.t([128, 1024], F32) for _ in range(2)]
    qh_ld = [cv.t([64, 2048], BF16) for _ in range(2)]
    of_sb = cv.t([128, 1024], F32)
    sqc = cv.t([128, 1024], F32)
    ssum = cv.t([128, 1], F32)
    rstd = cv.t([128, 1], F32)
    on1 = cv.t([128, 1024], F32)
    on_f = cv.t([128, 1024], F32)
    on_fm = cv.t([128, 8 * 512], BF16)
    gt_ld = cv.t([128, 8 * 512], F32)
    xt_ld = cv.t([128, 8 * 512], F32)
    ostage = [cv.t([128, 8 * 512], F32) for _ in range(2)]
    tmpc = cv.t([128, 512], F32)
    S_pair = cv.t([64, 1024], F32)
    Scor_f = cv.t([64, 1024], F32)
    Scor_b = cv.t([64, 1024], BF16)
    nw_row = cv.t([1, 1024], F32)
    nw_bc = cv.t([128, 1024], F32)
    assert max(ca.off, cb.off, cv.off) <= ARENA, (ca.off, cb.off, cv.off)

    pb = [stack.enter_context(nc.psum_tensor("pb%d" % i, [128, 512], F32))
          for i in range(8)]
    pbt = ["pb%d" % i for i in range(8)]

    A = ir.add

    def mm(bank_ap, lhsT, rhs, start, stop, reads, wtok, tag=""):
        A("tensor",
          (lambda e, o=bank_ap, l=lhsT, r=rhs, st=start, sp=stop:
           e.matmul(o, l, r, start=st, stop=sp)),
          reads=reads, writes=(wtok,), tag=tag)

    A("vector", lambda e: e.memset(ones_col[:], 1.0), writes=("ones_col",))
    A("vector", lambda e: e.memset(ones_row[:], 1.0), writes=("ones_row",))
    A("vector", lambda e: e.memset(S_f[:], 0.0), writes=("S_f",))
    A("vector", lambda e: e.memset(S_bf[:], 0.0), writes=("S_bf",))
    A("vector", lambda e: e.memset(PvecD[:], 1.0), writes=("PvecD",))
    A("sync", lambda e: e.dma_start(out=triu[:], in_=dtriu), writes=("triu",), slot="m0")
    A("sync", lambda e: e.dma_start(out=ident[:], in_=dident), writes=("ident",), slot="m1")
    A("sync", lambda e: e.dma_start(out=maskS[:], in_=dmaskS), writes=("maskS",), slot="m2")
    A("sync", lambda e: e.dma_start(out=maskC[:], in_=dmaskC), writes=("maskC",), slot="m3")
    A("sync", lambda e: e.dma_start(out=drow[:], in_=ddelta), writes=("drow",), slot="m4")
    A("sync", lambda e: e.dma_start(
        out=wg_f.rearrange("p (k c) -> p k c", k=8),
        in_=dwg.rearrange("(k p) c -> p k c", p=128)), writes=("wg_f",), slot="m5")
    for wdst, wtok, src in ((wq_b, "wq_b", dwq), (wk_b, "wk_b", dwk),
                            (wv_b, "wv_b", dwv)):
        A("sync", (lambda e, s=src: e.dma_start(
            out=wland.rearrange("p (k c) -> p k c", k=8),
            in_=s.rearrange("(k p) c -> p k c", p=128))),
          writes=("wland",), slot="wld")
        A("scalar", (lambda e, d=wdst: e.copy(out=d, in_=wland[:])),
          reads=("wland",), writes=(wtok,))
    A("sync", lambda e: e.dma_start(
        out=wland.rearrange("p (k c) -> p k c", k=8)[:, :, 0:16],
        in_=dwf.rearrange("(k p) c -> p k c", p=128)),
      writes=("wland",), slot="wld")
    A("scalar", lambda e: e.copy(
        out=wf_b.rearrange("p (k c) -> p k c", k=8),
        in_=wland.rearrange("p (k c) -> p k c", k=8)[:, :, 0:16]),
      reads=("wland",), writes=("wf_b",))
    mm(pb[2][0:128, 0:16], ones_row[:], drow[:], True, True,
       ("ones_row", "drow"), pbt[2], "deltab")
    A("vector", lambda e: e.tensor_copy(out=delta_b[:], in_=pb[2][0:128, 0:16]),
      reads=(pbt[2],), writes=("delta_b",))

    ir.barrier()

    for t in range(NTILE if "A" in phases else 0):
        xb = t % 2
        xtok = "xf%d" % xb
        xntok = "xnb%d" % xb
        A("sync", (lambda e, t=t, xb=xb: e.dma_start(
            out=xf[xb].rearrange("p (k n) -> p k n", k=8),
            in_=dx3[:, :, t * 512:(t + 1) * 512])),
          writes=(xtok,), slot=xtok)
        for k in range(8):
            mm(pb[6][0:1, 0:512], ones_col[:],
               xf[xb][:, k * 512:(k + 1) * 512],
               k == 0, k == 7, ("ones_col", xtok), pbt[6], "lnsum")
        for k in range(8):
            sq = sqb[k % 2]
            sqt = "sqb%d" % (k % 2)
            A("scalar", (lambda e, k=k, xb=xb, sq=sq: e.activation(
                out=sq[:], in_=xf[xb][:, k * 512:(k + 1) * 512], func=AF.Square)),
              reads=(xtok,), writes=(sqt,))
            mm(pb[7][0:1, 0:512], ones_col[:], sq[:], k == 0, k == 7,
               ("ones_col", sqt), pbt[7], "lnsq")
        A("vector", lambda e: e.tensor_scalar_mul(mrow[:], pb[6][0:1, 0:512], 1.0 / 1024.0),
          reads=(pbt[6],), writes=("mrow",))
        A("vector", lambda e: e.tensor_scalar_mul(vrow[:], pb[7][0:1, 0:512], 1.0 / 1024.0),
          reads=(pbt[7],), writes=("vrow",))
        A("vector", lambda e: e.tensor_tensor(out=m2row[:], in0=mrow[:], in1=mrow[:], op=ALU.mult),
          reads=("mrow",), writes=("m2row",))
        A("vector", lambda e: e.tensor_sub(vrow[:], vrow[:], m2row[:]),
          reads=("vrow", "m2row"), writes=("vrow",))
        A("vector", lambda e: e.tensor_scalar_add(vrow[:], vrow[:], 1e-5),
          reads=("vrow",), writes=("vrow",))
        A("scalar", lambda e: e.activation(out=srow[:], in_=vrow[:], func=AF.Sqrt),
          reads=("vrow",), writes=("srow",))
        A("vector", lambda e: e.reciprocal(out=rrow[:], in_=srow[:]),
          reads=("srow",), writes=("rrow",))
        A("vector", lambda e: e.tensor_tensor(out=brow[:], in0=mrow[:], in1=rrow[:], op=ALU.mult),
          reads=("mrow", "rrow"), writes=("brow",))
        A("vector", lambda e: e.tensor_scalar_mul(brow[:], brow[:], -1.0),
          reads=("brow",), writes=("brow",))
        mm(pb[2][:, 0:512], ones_row[:], rrow[:], True, True,
           ("ones_row", "rrow"), pbt[2], "bca")
        mm(pb[3][:, 0:512], ones_row[:], brow[:], True, True,
           ("ones_row", "brow"), pbt[3], "bcb")
        def _b8(big, small):
            return bass.broadcast_tensor_aps(
                big.rearrange("p (u n) -> p u n", u=8),
                small.rearrange("p (u n) -> p u n", u=1))
        A("vector", (lambda e, xb=xb: e.tensor_tensor(
            out=xnf.rearrange("p (u n) -> p u n", u=8),
            in0=_b8(xf[xb], pb[2][:, 0:512])[0],
            in1=_b8(xf[xb], pb[2][:, 0:512])[1], op=ALU.mult)),
          reads=(xtok, pbt[2]), writes=("xnf",))
        A("vector", (lambda e: e.tensor_tensor(
            out=xnf.rearrange("p (u n) -> p u n", u=8),
            in0=xnf.rearrange("p (u n) -> p u n", u=8),
            in1=_b8(xnf, pb[3][:, 0:512])[1], op=ALU.add)),
          reads=("xnf", pbt[3]), writes=("xnf",))
        A("scalar", (lambda e, xb=xb: e.copy(out=xnb[xb][:], in_=xnf[:])),
          reads=("xnf",), writes=(xntok,))
        A("pool", (lambda e, t=t, xb=xb: e.dma_start(
            out=xn_scr[:, :, t * 512:(t + 1) * 512],
            in_=xnb[xb].rearrange("p (k n) -> p k n", k=8))),
          reads=(xntok,), slot="xns%d" % xb)

        for cc_i in range(4):
            c = t * 4 + cc_i

            def xsl(k, cc_i=cc_i, xb=xb):
                return xnb[xb][:, k * 512 + cc_i * 128: k * 512 + cc_i * 128 + 128]

            def xslf(k, cc_i=cc_i):
                return xnf[:, k * 512 + cc_i * 128: k * 512 + cc_i * 128 + 128]

            for k in range(8):
                mm(pb[6][0:128, 0:16], xsl(k), wf_b[:, k * 16:(k + 1) * 16],
                   k == 0, k == 7, (xntok, "wf_b"), pbt[6], "f")
                mm(pb[7][0:128, 0:8], xslf(k), wg_f[:, k * 8:(k + 1) * 8],
                   k == 0, k == 7, ("xnf", "wg_f"), pbt[7], "lg")
            A("vector", lambda e: e.tensor_tensor(out=zf_sb[:], in0=pb[6][0:128, 0:16],
                                                  in1=delta_b[:], op=ALU.add),
              reads=(pbt[6], "delta_b"), writes=("zf",))
            A("scalar", lambda e: e.activation(out=az_sb[:], in_=zf_sb[:], func=AF.Abs),
              reads=("zf",), writes=("az",))
            A("scalar", lambda e: e.activation(out=e1f_sb[:], in_=az_sb[:], func=AF.Exp, scale=-1.0),
              reads=("az",), writes=("e1f",))
            A("scalar", lambda e: e.activation(out=l1f_sb[:], in_=e1f_sb[:], func=AF.Ln, bias=1.0),
              reads=("e1f",), writes=("l1f",))
            A("vector", lambda e: e.tensor_single_scalar(out=zm_sb[:], in_=zf_sb[:],
                                                       scalar=0.0, op=ALU.min),
              reads=("zf",), writes=("zm",))
            A("pool", lambda e: e.tensor_sub(lf_sb[:], zm_sb[:], l1f_sb[:]),
              reads=("zm", "l1f"), writes=("lf",))
            A("vector", lambda e: e.tensor_copy(out=lgs_sb[:], in_=pb[7][0:128, 0:8]),
              reads=(pbt[7],), writes=("lgs",))
            A("vector", lambda e: e.reduce_max(out=m1g[:], in_=lgs_sb[:], axis=AX.X),
              reads=("lgs",), writes=("m1g",))
            A("vector", lambda e: e.tensor_single_scalar(out=e1g[:], in_=lgs_sb[:],
                                                       scalar=m1g[:, 0:1], op=ALU.is_ge),
              reads=("lgs", "m1g"), writes=("e1g",))
            A("vector", lambda e: e.scalar_tensor_tensor(out=lg2[:], in0=e1g[:], scalar=-1e30,
                                                       in1=lgs_sb[:], op0=ALU.mult, op1=ALU.add),
              reads=("e1g", "lgs"), writes=("lg2",))
            A("vector", lambda e: e.reduce_max(out=m2g[:], in_=lg2[:], axis=AX.X),
              reads=("lg2",), writes=("m2g",))
            A("vector", lambda e: e.tensor_single_scalar(out=e2g[:], in_=lg2[:],
                                                       scalar=m2g[:, 0:1], op=ALU.is_ge),
              reads=("lg2", "m2g"), writes=("e2g",))
            A("scalar", lambda e: e.activation(out=w1g[:], in_=m2g[:], func=AF.Sigmoid,
                                               bias=m1g[:, 0:1], scale=-1.0),
              reads=("m2g", "m1g"), writes=("w1g",))
            A("vector", lambda e: e.tensor_scalar(out=w2g[:], in0=w1g[:], scalar1=-1.0,
                                                scalar2=1.0, op0=ALU.mult, op1=ALU.add),
              reads=("w1g",), writes=("w2g",))
            A("vector", lambda e: e.tensor_scalar_mul(gts_tm[:], e1g[:], w1g[:, 0:1]),
              reads=("e1g", "w1g"), writes=("gts_tm",))
            A("vector", lambda e: e.scalar_tensor_tensor(out=gts_tm[:], in0=e2g[:],
                                                       scalar=w2g[:, 0:1], in1=gts_tm[:],
                                                       op0=ALU.mult, op1=ALU.add),
              reads=("e2g", "w2g", "gts_tm"), writes=("gts_tm",))
            if dbg:
                A("pool", (lambda e, c=c: e.dma_start(out=dbg_lgs[c], in_=lgs_sb[:])),
                  reads=("lgs",), slot="dbg0")
                A("pool", (lambda e, c=c: e.dma_start(out=dbg_gts[c], in_=gts_tm[:])),
                  reads=("gts_tm",), slot="dbg1")
                A("pool", (lambda e, c=c: e.dma_start(out=dbg_lf[c], in_=lf_sb[:])),
                  reads=("lf",), slot="dbg2")
                A("pool", (lambda e, c=c: e.dma_start(out=dbg_elb[c], in_=expLb[:])),
                  reads=("expLb",), slot="dbg3")
            for k in range(8):
                mm(pb[0][:], xsl(k), wq_b[:, k * 1024:k * 1024 + 512],
                   k == 0, k == 7, (xntok, "wq_b"), pbt[0], "q0")
                mm(pb[1][:], xsl(k), wq_b[:, k * 1024 + 512:(k + 1) * 1024],
                   k == 0, k == 7, (xntok, "wq_b"), pbt[1], "q1")
            A("scalar", lambda e: e.activation(out=q_tm[:, 0:512], in_=pb[0][:], func=AF.Silu),
              reads=(pbt[0],), writes=("q_tm",))
            A("scalar", lambda e: e.activation(out=q_tm[:, 512:1024], in_=pb[1][:], func=AF.Silu),
              reads=(pbt[1],), writes=("q_tm",))
            for k in range(8):
                mm(pb[2][:], xsl(k), wk_b[:, k * 1024:k * 1024 + 512],
                   k == 0, k == 7, (xntok, "wk_b"), pbt[2], "k0")
                mm(pb[3][:], xsl(k), wk_b[:, k * 1024 + 512:(k + 1) * 1024],
                   k == 0, k == 7, (xntok, "wk_b"), pbt[3], "k1")
            A("scalar", lambda e: e.activation(out=k_tm[:, 0:512], in_=pb[2][:], func=AF.Silu),
              reads=(pbt[2],), writes=("k_tm",))
            A("scalar", lambda e: e.activation(out=k_tm[:, 512:1024], in_=pb[3][:], func=AF.Silu),
              reads=(pbt[3],), writes=("k_tm",))
            for k in range(8):
                mm(pb[0][:], xsl(k), wv_b[:, k * 1024:k * 1024 + 512],
                   k == 0, k == 7, (xntok, "wv_b"), pbt[0], "v0")
                mm(pb[1][:], xsl(k), wv_b[:, k * 1024 + 512:(k + 1) * 1024],
                   k == 0, k == 7, (xntok, "wv_b"), pbt[1], "v1")
            A("scalar", lambda e: e.activation(out=v_tm[:, 0:512], in_=pb[0][:], func=AF.Copy),
              reads=(pbt[0],), writes=("v_tm",))
            A("scalar", lambda e: e.activation(out=v_tm[:, 512:1024], in_=pb[1][:], func=AF.Copy),
              reads=(pbt[1],), writes=("v_tm",))
            mm(pb[6][0:128, 0:16], triu[:], lf_sb[:], True, True,
               ("triu", "lf"), pbt[6], "cumsum")
            A("scalar", lambda e: e.activation(out=expLb[:], in_=pb[6][0:128, 0:16], func=AF.Exp),
              reads=(pbt[6],), writes=("expLb",))
            A("scalar", lambda e: e.activation(out=emLb[:], in_=pb[6][0:128, 0:16],
                                               func=AF.Exp, scale=-1.0),
              reads=(pbt[6],), writes=("emLb",))
            mm(pb[7][0:1, 0:16], ones_col[:], lf_sb[:], True, True,
               ("ones_col", "lf"), pbt[7], "tot")
            A("vector", lambda e: e.tensor_copy(out=totR[:], in_=pb[7][0:1, 0:16]),
              reads=(pbt[7],), writes=("totR",))
            mm(pb[7][0:128, 0:16], ones_row[:], totR[:], True, True,
               ("ones_row", "totR"), pbt[7], "totb")
            A("scalar", lambda e: e.activation(out=eTotD[:], in_=pb[7][0:128, 0:16], func=AF.Exp),
              reads=(pbt[7],), writes=("eTotD",))
            def _bh(big, small, hh=16):
                return bass.broadcast_tensor_aps(
                    big.rearrange("p (h n) -> p h n", h=hh),
                    small.rearrange("p (h u) -> p h u", u=1))
            A("vector", (lambda e: e.tensor_tensor(
                out=qd_tm.rearrange("p (h n) -> p h n", h=16),
                in0=_bh(q_tm, expLb)[0], in1=_bh(q_tm, expLb)[1], op=ALU.mult)),
              reads=("q_tm", "expLb"), writes=("qd_tm",))
            A("vector", (lambda e: e.tensor_tensor(
                out=kd_tmf.rearrange("p (h n) -> p h n", h=16),
                in0=_bh(k_tm, emLb)[0], in1=_bh(k_tm, emLb)[1], op=ALU.mult)),
              reads=("k_tm", "emLb"), writes=("kd_tmf",))
            A("vector", (lambda e: e.tensor_tensor(
                out=kd_tmb.rearrange("p (h n) -> p h n", h=16),
                in0=_bh(k_tm, emLb)[0], in1=_bh(k_tm, emLb)[1], op=ALU.mult)),
              reads=("k_tm", "emLb"), writes=("kd_tmb",))
            for grp in range(4):
                bank = 2 + (grp % 2)
                for j in range(4):
                    h = grp * 4 + j
                    A("tensor", (lambda e, h=h, j=j, bank=bank: e.transpose(
                        pb[bank][0:64, j * 128:(j + 1) * 128],
                        qd_tm[:, h * 64:(h + 1) * 64], ident[:])),
                      reads=("qd_tm", "ident"), writes=(pbt[bank],))
                A("scalar", (lambda e, grp=grp, bank=bank: e.copy(
                    out=qd_fm[:, grp * 512:(grp + 1) * 512], in_=pb[bank][0:64, 0:512])),
                  reads=(pbt[bank],), writes=("qd_fm",))
            for grp in range(4):
                bank = 2 + (grp % 2)
                for j in range(4):
                    h = grp * 4 + j
                    A("tensor", (lambda e, h=h, j=j, bank=bank: e.transpose(
                        pb[bank][0:64, j * 128:(j + 1) * 128],
                        kd_tmf[:, h * 64:(h + 1) * 64], ident[:])),
                      reads=("kd_tmf", "ident"), writes=(pbt[bank],))
                A("vector", (lambda e, grp=grp, bank=bank: e.tensor_copy(
                    out=kd_fm[:, grp * 512:(grp + 1) * 512], in_=pb[bank][0:64, 0:512])),
                  reads=(pbt[bank],), writes=("kd_fm",))
            qhb = c % 2
            A("vector", lambda e: e.tensor_scalar_mul(PM[:], PvecD[:], maskS[:, 0:1]),
              reads=("PvecD", "maskS"), writes=("PM",))
            A("vector", (lambda e, qhb=qhb: e.tensor_tensor(
                out=qh_fm[qhb].rearrange("p (h n) -> p h n", h=16),
                in0=_bh(qd_fm, PM[0:64, :])[0],
                in1=_bh(qd_fm, PM[0:64, :])[1], op=ALU.mult)),
              reads=("qd_fm", "PM"), writes=("qh_fm%d" % qhb,))
            A("pool", (lambda e, c=c, qhb=qhb: e.dma_start(
                out=qh_scr[c], in_=qh_fm[qhb][:])),
              reads=("qh_fm%d" % qhb,), slot="qs%d" % qhb)
            ob = c % 2
            for grp in range(4):
                abank = 2 + (grp % 2)
                aslot = grp % 2
                for j in range(4):
                    h = grp * 4 + j
                    mm(pb[abank][:, j * 128:(j + 1) * 128],
                       kd_fm[:, h * 128:(h + 1) * 128],
                       qd_fm[:, h * 128:(h + 1) * 128],
                       True, True, ("kd_fm", "qd_fm"), pbt[abank], "amm")
                A("vector", (lambda e, abank=abank, aslot=aslot: e.tensor_tensor(
                    out=a_bf4[aslot].rearrange("p (u n) -> p u n", u=4),
                    in0=pb[abank][:, 0:512].rearrange("p (u n) -> p u n", u=4),
                    in1=bass.broadcast_tensor_aps(
                        pb[abank][:, 0:512].rearrange("p (u n) -> p u n", u=4),
                        triu.rearrange("p (u n) -> p u n", u=1))[1],
                    op=ALU.mult)),
                  reads=(pbt[abank], "triu"), writes=("a_bf%d" % aslot,))
                for j in range(4):
                    h = grp * 4 + j
                    obank = 4 + (h // 8)
                    ocol = (h % 8) * 64
                    ubank = 6 + (h // 8)
                    mm(pb[obank][:, ocol:ocol + 64],
                       a_bf4[aslot][:, j * 128:(j + 1) * 128],
                       v_tm[:, h * 64:(h + 1) * 64], True, False,
                       ("a_bf%d" % aslot, "v_tm"), pbt[obank], "ointra")
                    mm(pb[obank][:, ocol:ocol + 64],
                       qd_fm[:, h * 128:(h + 1) * 128],
                       S_bf[:, h * 64:(h + 1) * 64],
                       False, True, ("qd_fm", "S_bf"), pbt[obank], "ointer")
                    mm(pb[ubank][0:64, (h % 8) * 64:(h % 8) * 64 + 64],
                       kd_tmb[:, h * 64:(h + 1) * 64],
                       v_tm[:, h * 64:(h + 1) * 64], True, True,
                       ("kd_tmb", "v_tm"), pbt[ubank], "umm")
            A("vector", lambda e: e.tensor_tensor(
                out=S_f[:, 0:512], in0=S_f[:, 0:512], in1=pb[6][0:64, 0:512],
                op=ALU.add),
              reads=("S_f", pbt[6]), writes=("S_f",))
            A("vector", lambda e: e.tensor_tensor(
                out=S_f[:, 512:1024], in0=S_f[:, 512:1024], in1=pb[7][0:64, 0:512],
                op=ALU.add),
              reads=("S_f", pbt[7]), writes=("S_f",))
            A("vector", lambda e: e.tensor_tensor(
                out=S_f.rearrange("p (h n) -> p h n", h=16),
                in0=_bh(S_f, eTotD[0:64, :])[0],
                in1=_bh(S_f, eTotD[0:64, :])[1], op=ALU.mult),
              reads=("S_f", "eTotD"), writes=("S_f",))
            A("vector", (lambda e, ob=ob: e.tensor_copy(out=o_tm[ob][:, 0:512], in_=pb[4][:])),
              reads=(pbt[4],), writes=("o_tm%d" % ob,))
            A("vector", (lambda e, ob=ob: e.tensor_copy(out=o_tm[ob][:, 512:1024], in_=pb[5][:])),
              reads=(pbt[5],), writes=("o_tm%d" % ob,))
            A("pool", (lambda e, c=c, ob=ob: e.dma_start(out=o_scr[c], in_=o_tm[ob][:])),
              reads=("o_tm%d" % ob,), slot="os%d" % ob)
            A("vector", lambda e: e.tensor_copy(out=S_bf[:], in_=S_f[:]),
              reads=("S_f",), writes=("S_bf",))
            A("vector", lambda e: e.tensor_tensor(out=PvecD[:], in0=PvecD[:],
                                                  in1=eTotD[:], op=ALU.mult),
              reads=("PvecD", "eTotD"), writes=("PvecD",))
            A("tensor", lambda e: e.transpose(pb[2][0:8, 0:128], gts_tm[:], ident[:]),
              reads=("gts_tm", "ident"), writes=(pbt[2],))
            A("vector", (lambda e, cc_i=cc_i: e.tensor_copy(
                out=gtsT_t[:, cc_i * 128:(cc_i + 1) * 128], in_=pb[2][0:8, 0:128])),
              reads=(pbt[2],), writes=("gtsT_t",))
        A("pool", (lambda e, t=t: e.dma_start(
            out=gts_scr[:, t * 512:(t + 1) * 512], in_=gtsT_t[:])),
          reads=("gtsT_t",), slot="gts")

    if "A" not in phases:
        phases = phases
    A("vector", lambda e: e.tensor_scalar_mul(Scon[:], S_f[:], maskC[0:64, 0:1]),
      reads=("S_f", "maskC"), writes=("Scon",))
    A("pool", lambda e: e.dma_start(out=cc_in, in_=Scon[:]),
      reads=("Scon",), writes=("cc_in",), slot="ccin")
    A("pool", lambda e: e.collective_compute(
        "AllReduce", ALU.add, replica_groups=[[0, 1], [2, 3], [4, 5], [6, 7]],
        ins=[cc_in.opt()], outs=[cc_out.opt()]),
      reads=("cc_in",), writes=("cc_out",), slot="cc", inc=1)

    ir.barrier()

    A("sync", lambda e: e.dma_start(out=b1_sb[:], in_=db1), writes=("b1_sb",), slot="m0")
    A("sync", lambda e: e.dma_start(out=b2_sb[:], in_=db2), writes=("b2_sb",), slot="m1")
    A("sync", lambda e: e.dma_start(out=gsel_sb[:], in_=dgsel), writes=("gsel_sb",), slot="m2")

    for g in range(NG if "B" in phases else 0):
        gxb = g % 2
        A("sync", (lambda e, g=g, gxb=gxb: e.dma_start(
            out=xng[gxb].rearrange("p (k n) -> p k n", k=8),
            in_=xn_scr[:, :, g * GT:(g + 1) * GT])),
          writes=("xng%d" % gxb,), slot="xng%d" % gxb)
        A("sync", (lambda e, g=g: e.dma_start(
            out=gtsT_g[:], in_=gts_scr[:, g * GT:(g + 1) * GT])),
          writes=("gtsT_g",), slot="gtg")
        for tt in range(2):
            for hc in range(8):
                bank = 4 + (2 * tt + hc) % 4
                mm(pb[bank][:], b2_sb[:, hc * 128:(hc + 1) * 128],
                   gtsT_g[:, tt * 512:(tt + 1) * 512], True, True,
                   ("b2_sb", "gtsT_g"), pbt[bank], "yinit")
                A("vector", (lambda e, tt=tt, hc=hc, bank=bank: e.tensor_copy(
                    out=y_sb[:, hc * 1024 + tt * 512: hc * 1024 + tt * 512 + 512],
                    in_=pb[bank][:])),
                  reads=(pbt[bank],), writes=("y_sb",))
        for e_i in range(NE):
            for m in range(32):
                ws = m % 3
                A("sync", (lambda e, e_i=e_i, m=m, ws=ws: e.dma_start(
                    out=w1l[ws].rearrange("p (k c) -> p k c", k=8),
                    in_=dw1[e_i].rearrange("(k p) c -> p k c", p=128)
                    [:, :, m * 128:(m + 1) * 128])),
                  writes=("w1l%d" % ws,), slot="w1l%d" % ws)
                A("vector", (lambda e, ws=ws: e.tensor_copy(out=w1b[ws], in_=w1l[ws][:])),
                  reads=("w1l%d" % ws,), writes=("w1b%d" % ws,))
                bA = (m % 2) * 2
                bB = bA + 1
                for k in range(8):
                    mm(pb[bA][:], w1b[ws][:, k * 128:(k + 1) * 128],
                       xng[gxb][:, k * 1024:k * 1024 + 512], k == 0, k == 7,
                       ("w1b%d" % ws, "xng%d" % gxb), pbt[bA], "h0")
                    mm(pb[bB][:], w1b[ws][:, k * 128:(k + 1) * 128],
                       xng[gxb][:, k * 1024 + 512:(k + 1) * 1024], k == 0, k == 7,
                       ("w1b%d" % ws, "xng%d" % gxb), pbt[bB], "h1")
                bias = b1_sb[:, e_i * 32 + m: e_i * 32 + m + 1]
                A("scalar", (lambda e, m=m, bA=bA, bias=bias: e.activation(
                    out=ht[:, m * 1024:m * 1024 + 512], in_=pb[bA][:],
                    func=AF.Gelu, bias=bias)),
                  reads=(pbt[bA], "b1_sb"), writes=("ht",))
                A("scalar", (lambda e, m=m, bB=bB, bias=bias: e.activation(
                    out=ht[:, m * 1024 + 512:(m + 1) * 1024], in_=pb[bB][:],
                    func=AF.Gelu, bias=bias)),
                  reads=(pbt[bB], "b1_sb"), writes=("ht",))
            for tt in range(2):
                mm(pb[tt][:], gsel_sb[:, e_i * 128:(e_i + 1) * 128],
                   gtsT_g[:, tt * 512:(tt + 1) * 512], True, True,
                   ("gsel_sb", "gtsT_g"), pbt[tt], "gbmm")
                A("vector", (lambda e, tt=tt: e.tensor_copy(out=gb_sb[tt][:], in_=pb[tt][:])),
                  reads=(pbt[tt],), writes=("gb%d" % tt,))
            for hc in range(8):
                w2s = hc % 2
                for hf in range(2):
                    A("sync", (lambda e, e_i=e_i, hc=hc, hf=hf, w2s=w2s: e.dma_start(
                        out=w2l[w2s].rearrange("p (k c) -> p k c", k=16),
                        in_=dw2[e_i].rearrange("(m p) c -> p m c", p=128)
                        [:, hf * 16:(hf + 1) * 16, hc * 128:(hc + 1) * 128])),
                      writes=("w2l%d" % w2s,), slot="w2l%d" % w2s)
                    A("scalar", (lambda e, hf=hf, w2s=w2s: e.copy(
                        out=w2b[w2s][:, hf * 2048:(hf + 1) * 2048], in_=w2l[w2s][:])),
                      reads=("w2l%d" % w2s,), writes=("w2b%d" % w2s,))
                bP = 4 + (hc % 2) * 2
                bQ = bP + 1
                for m in range(32):
                    mm(pb[bP][:], w2b[w2s][:, m * 128:(m + 1) * 128],
                       ht[:, m * 1024:m * 1024 + 512], m == 0, m == 31,
                       ("w2b%d" % w2s, "ht"), pbt[bP], "o0")
                    mm(pb[bQ][:], w2b[w2s][:, m * 128:(m + 1) * 128],
                       ht[:, m * 1024 + 512:(m + 1) * 1024], m == 0, m == 31,
                       ("w2b%d" % w2s, "ht"), pbt[bQ], "o1")
                for tt, bx in ((0, bP), (1, bQ)):
                    A("vector", (lambda e, bx=bx, tt=tt: e.tensor_tensor(
                        out=tmpb[tt][:], in0=pb[bx][:], in1=gb_sb[tt][:], op=ALU.mult)),
                      reads=(pbt[bx], "gb%d" % tt), writes=("tmpb0",))
                    ys = y_sb[:, hc * 1024 + tt * 512: hc * 1024 + tt * 512 + 512]
                    A("vector", (lambda e, ys=ys, tt=tt: e.tensor_tensor(
                        out=ys, in0=ys, in1=tmpb[tt][:], op=ALU.add)),
                      reads=("tmpb0", "y_sb"), writes=("y_sb",))
        for hc in range(8):
            gsl = hc % 2
            A("scalar", (lambda e, hc=hc, gsl=gsl: e.activation(
                out=gstage[gsl][:], in_=y_sb[:, hc * 1024:(hc + 1) * 1024], func=AF.Silu)),
              reads=("y_sb",), writes=("gstage%d" % gsl,))
            A("pool", (lambda e, g=g, hc=hc, gsl=gsl: e.dma_start(
                out=g_scr[hc, 2 * g:2 * g + 2].rearrange("a p n -> p a n"),
                in_=gstage[gsl].rearrange("p (a n) -> p a n", a=2))),
              reads=("gstage%d" % gsl,), slot="gs%d" % gsl)

    ir.barrier()

    A("sync", lambda e: e.dma_start(
        out=wlandc.rearrange("p (k c) -> p k c", k=8),
        in_=dwo.rearrange("(k p) c -> p k c", p=128)),
      writes=("wlandc",), slot="wld")
    A("scalar", lambda e: e.copy(out=wo_b, in_=wlandc[:]),
      reads=("wlandc",), writes=("wo_b",))
    A("sync", lambda e: e.dma_start(out=S_pair[:], in_=cc_out),
      reads=("cc_out",), writes=("S_pair",), slot="sld")
    A("vector", lambda e: e.tensor_scalar_mul(Scor_f[:], S_pair[:], maskS[0:64, 0:1]),
      reads=("S_pair", "maskS"), writes=("Scor_f",))
    A("vector", lambda e: e.tensor_copy(out=Scor_b[:], in_=Scor_f[:]),
      reads=("Scor_f",), writes=("Scor_b",))
    A("sync", lambda e: e.dma_start(out=nw_row[:], in_=dnw),
      writes=("nw_row",), slot="m3")
    mm(pb[2][:], ones_row[:], nw_row[:, 0:512], True, True,
       ("ones_row", "nw_row"), pbt[2], "nw0")
    mm(pb[3][:], ones_row[:], nw_row[:, 512:1024], True, True,
       ("ones_row", "nw_row"), pbt[3], "nw1")
    A("vector", lambda e: e.tensor_copy(out=nw_bc[:, 0:512], in_=pb[2][:]),
      reads=(pbt[2],), writes=("nw_bc",))
    A("vector", lambda e: e.tensor_copy(out=nw_bc[:, 512:1024], in_=pb[3][:]),
      reads=(pbt[3],), writes=("nw_bc",))

    for t in range(NTILE if "C" in phases else 0):
        A("sync", (lambda e, t=t: e.dma_start(
            out=gt_ld.rearrange("p (k n) -> p k n", k=8),
            in_=g_scr[:, t].rearrange("h p n -> p h n"))),
          writes=("gt_ld",), slot="gld")
        A("sync", (lambda e, t=t: e.dma_start(
            out=xt_ld.rearrange("p (k n) -> p k n", k=8),
            in_=dx3[:, :, t * 512:(t + 1) * 512])),
          writes=("xt_ld",), slot="xtl")
        for cc_i in range(4):
            c = t * 4 + cc_i
            lb = c % 2
            A("sync", (lambda e, c=c, lb=lb: e.dma_start(out=o_ld[lb][:], in_=o_scr[c])),
              writes=("o_ld%d" % lb,), slot="old%d" % lb)
            A("sync", (lambda e, c=c, lb=lb: e.dma_start(out=qh_ld[lb][:], in_=qh_scr[c])),
              writes=("qh_ld%d" % lb,), slot="qld%d" % lb)
            for h in range(16 if "corr" in CFLAGS else 0):
                obank = h // 8
                ocol = (h % 8) * 64
                mm(pb[obank][:, ocol:ocol + 64],
                   qh_ld[lb][:, h * 128:(h + 1) * 128],
                   Scor_b[:, h * 64:(h + 1) * 64],
                   True, True, ("qh_ld%d" % lb, "Scor_b"), pbt[obank], "corr")
            if "corr" in CFLAGS:
                A("vector", (lambda e, lb=lb: e.tensor_tensor(
                    out=of_sb[:, 0:512], in0=o_ld[lb][:, 0:512], in1=pb[0][:, 0:512],
                    op=ALU.add)),
                  reads=("o_ld%d" % lb, pbt[0]), writes=("of",))
                A("vector", (lambda e, lb=lb: e.tensor_tensor(
                    out=of_sb[:, 512:1024], in0=o_ld[lb][:, 512:1024], in1=pb[1][:, 0:512],
                    op=ALU.add)),
                  reads=("o_ld%d" % lb, pbt[1]), writes=("of",))
            else:
                A("vector", (lambda e, lb=lb: e.tensor_copy(out=of_sb[:, 0:1024],
                                                            in_=o_ld[lb][:, 0:1024])),
                  reads=("o_ld%d" % lb,), writes=("of",))
            if "rms" not in CFLAGS:
                A("vector", lambda e: e.tensor_copy(out=on_f[:], in_=of_sb[:]),
                  reads=("of",), writes=("on_f",))
            if "rms" not in CFLAGS:
                pass
            _n_rms = 1 if "rms" in CFLAGS else 0
            for _ in range(_n_rms):
                A("scalar", lambda e: e.activation(out=sqc[:], in_=of_sb[:], func=AF.Square),
                  reads=("of",), writes=("sqc",))
            for _ in range(_n_rms):
                A("vector", lambda e: e.reduce_sum(out=ssum[:], in_=sqc[:], axis=AX.X),
                  reads=("sqc",), writes=("ssum",))
            for _ in range(_n_rms):
                A("vector", lambda e: e.tensor_scalar(out=rstd[:], in0=ssum[:], scalar1=1.0 / 1024.0,
                                                      scalar2=1e-6, op0=ALU.mult, op1=ALU.add),
                  reads=("ssum",), writes=("rstd",))
            for _ in range(_n_rms):
                A("scalar", lambda e: e.activation(out=rstd[:], in_=rstd[:], func=AF.Sqrt),
                  reads=("rstd",), writes=("rstd",))
            for _ in range(_n_rms):
                A("vector", lambda e: e.reciprocal(out=rstd[:], in_=rstd[:]),
                  reads=("rstd",), writes=("rstd",))
            for _ in range(_n_rms):
                A("vector", lambda e: e.tensor_scalar_mul(on1[:], of_sb[:], rstd[:, 0:1]),
                  reads=("of", "rstd"), writes=("on1",))
            for _ in range(_n_rms):
                A("vector", lambda e: e.tensor_tensor(out=on_f[:], in0=on1[:], in1=nw_bc[:],
                                                      op=ALU.mult),
                  reads=("on1", "nw_bc"), writes=("on_f",))
            for blk in range(8):
                bank = 4 + (blk % 2)
                A("tensor", (lambda e, blk=blk, bank=bank: e.transpose(
                    pb[bank][:, 0:128], on_f[:, blk * 128:(blk + 1) * 128], ident[:])),
                  reads=("on_f", "ident"), writes=(pbt[bank],))
                A("vector", (lambda e, blk=blk, bank=bank, cc_i=cc_i: e.tensor_copy(
                    out=on_fm[:, blk * 512 + cc_i * 128: blk * 512 + cc_i * 128 + 128],
                    in_=pb[bank][:, 0:128])),
                  reads=(pbt[bank],), writes=("on_fm",))
        osb = t % 2
        for fc in range(8):
            bank = 2 + (fc % 2)
            for k in range(8):
                mm(pb[bank][:], wo_b[:, k * 1024 + fc * 128: k * 1024 + fc * 128 + 128],
                   on_fm[:, k * 512:(k + 1) * 512], k == 0, k == 7,
                   ("wo_b", "on_fm"), pbt[bank], "sproj")
            A("vector", (lambda e, fc=fc, bank=bank: e.tensor_tensor(
                out=tmpc[:], in0=pb[bank][:], in1=gt_ld[:, fc * 512:(fc + 1) * 512],
                op=ALU.mult)),
              reads=(pbt[bank], "gt_ld"), writes=("tmpc",))
            A("vector", (lambda e, fc=fc, osb=osb: e.tensor_tensor(
                out=ostage[osb][:, fc * 512:(fc + 1) * 512], in0=tmpc[:],
                in1=xt_ld[:, fc * 512:(fc + 1) * 512], op=ALU.add)),
              reads=("tmpc", "xt_ld"), writes=("ostage%d" % osb,))
        A("pool", (lambda e, t=t, osb=osb: e.dma_start(
            out=dout3[:, :, t * 512:(t + 1) * 512],
            in_=ostage[osb].rearrange("p (k n) -> p k n", k=8))),
          reads=("ostage%d" % osb,), slot="outs%d" % osb)

    ir.barrier()

    ir.resolve()
    chain, slots = {}, {}
    with contextlib.ExitStack() as semstack:
        for e in ENGINES:
            chain[e] = semstack.enter_context(nc.semaphore("c_" + e))
        for s in sorted(ir.slot_count):
            slots[s] = semstack.enter_context(nc.semaphore("s_" + s))
        with nc.Block() as block:
            @block.tensor
            def _(eng):
                ir.emit_one("tensor", eng, chain, slots)

            @block.vector
            def _(eng):
                ir.emit_one("vector", eng, chain, slots)

            @block.scalar
            def _(eng):
                ir.emit_one("scalar", eng, chain, slots)

            @block.sync
            def _(eng):
                ir.emit_one("sync", eng, chain, slots)

            @block.gpsimd
            def _(eng):
                ir.emit_one("pool", eng, chain, slots)
    stack.close()
    return nc


def kernel(L0_attn_x, L0_attn_Wq, L0_attn_Wk, L0_attn_Wv, L0_attn_Wo,
           L0_attn_Wf, L0_attn_delta, L0_attn_norm_weight, L0_ffn_Wg,
           L0_ffn_W1, L0_ffn_b1, L0_ffn_W2, L0_ffn_b2):
    f32 = np.float32
    x = np.asarray(L0_attn_x, f32)
    wqT = np.ascontiguousarray(np.asarray(L0_attn_Wq, f32).T)
    wkT = np.ascontiguousarray(np.asarray(L0_attn_Wk, f32).T)
    wvT = np.ascontiguousarray(np.asarray(L0_attn_Wv, f32).T)
    woT = np.ascontiguousarray(np.asarray(L0_attn_Wo, f32).T)
    wfT = np.ascontiguousarray(np.asarray(L0_attn_Wf, f32).T)
    wgT = np.ascontiguousarray(np.asarray(L0_ffn_Wg, f32).T)
    w1 = np.ascontiguousarray(np.asarray(L0_ffn_W1, f32))
    w2 = np.ascontiguousarray(np.asarray(L0_ffn_W2, f32))
    b1p = np.ascontiguousarray(
        np.asarray(L0_ffn_b1, f32).reshape(8, 32, 128).transpose(2, 0, 1).reshape(128, 256))
    b2 = np.ascontiguousarray(np.asarray(L0_ffn_b2, f32))
    delta = np.ascontiguousarray(np.asarray(L0_attn_delta, f32).reshape(1, 16))
    nw = np.ascontiguousarray(np.asarray(L0_attn_norm_weight, f32).reshape(1, 1024))
    triu = np.ascontiguousarray(np.triu(np.ones((128, 128), f32)))
    ident = np.eye(128, dtype=f32)
    gsel = np.zeros((8, 1024), f32)
    for e in range(8):
        gsel[e, e * 128:(e + 1) * 128] = 1.0

    nc = build_program()
    in_maps = []
    for c in range(8):
        b, half = c // 2, c % 2
        xc = np.ascontiguousarray(x[b, half * 4096:(half + 1) * 4096, :].T)
        mS = np.full((128, 1), 1.0 if half == 1 else 0.0, f32)
        mC = np.full((128, 1), 1.0 if half == 0 else 0.0, f32)
        in_maps.append(dict(
            x=xc, wqT=wqT, wkT=wkT, wvT=wvT, woT=woT, wfT=wfT, wgT=wgT,
            w1=w1, w2=w2, b1p=b1p, b2=b2, delta=delta, nw=nw, triu=triu,
            ident=ident, gsel=gsel, maskS=mS, maskC=mC))
    res = run_bass_kernel_spmd(nc, in_maps, core_ids=list(range(8)))
    out = np.empty((4, 8192, 1024), f32)
    for c in range(8):
        b, half = c // 2, c % 2
        out[b, half * 4096:(half + 1) * 4096, :] = res.results[c]["out"].T
    return out


# version 07 (final)
# speedup vs baseline: 1.4608x; 1.2773x over previous
import contextlib
import numpy as np
import ml_dtypes
import concourse.bass as bass
import concourse.mybir as mybir
from concourse.bass_utils import run_bass_kernel_spmd

F32 = mybir.dt.float32
BF16 = mybir.dt.bfloat16
AF = mybir.ActivationFunctionType
ALU = mybir.AluOpType
AX = mybir.AxisListType

B_, D, H = 4, 1024, 16
TPC = 4096
NTILE = 8
NCHUNK = 32
NE, INTER = 8, 4096
NG = 4
GT = TPC // NG
CAP = 1536
NSEG = NE * CAP

ENGINES = ("tensor", "vector", "scalar", "sync", "pool")
CFLAGS = {"corr", "rms", "tp", "sproj", "comb"}


class IR:

    def __init__(self):
        self.instrs = []
        self.last_write = {}
        self.last_read = {}
        self.slot_count = {}

    def add(self, eng, emit, reads=(), writes=(), slot=None, inc=16, tag=""):
        iid = len(self.instrs)
        writes = tuple(writes)
        if slot is not None:
            writes = writes + ("__slot_" + slot,)
        deps = set()
        for t in reads:
            w = self.last_write.get(t)
            if w is not None:
                deps.add(w)
        for t in writes:
            w = self.last_write.get(t)
            if w is not None:
                deps.add(w)
            for r in self.last_read.get(t, {}).values():
                deps.add(r)
        deps.discard(iid)
        rec = dict(iid=iid, eng=eng, emit=emit, slot=slot, deps=deps, tag=tag,
                   flag=False, semval=None, dmaval=None, inc=inc)
        if slot is not None:
            self.slot_count[slot] = self.slot_count.get(slot, 0) + inc
            rec["dmaval"] = self.slot_count[slot]
        self.instrs.append(rec)
        for t in reads:
            self.last_read.setdefault(t, {})[eng] = iid
        for t in writes:
            self.last_write[t] = iid
            self.last_read[t] = {}
        return iid

    def barrier(self):
        iid = len(self.instrs)
        self.instrs.append(dict(iid=iid, eng="__bar__", emit=None, slot=None,
                                deps=set(), tag="barrier", flag=False,
                                semval=None, dmaval=None))
        self.last_write = {}
        self.last_read = {}
        return iid

    def resolve(self):
        for rec in self.instrs:
            if rec["eng"] == "__bar__":
                continue
            keep = {}
            for d in rec["deps"]:
                p = self.instrs[d]
                if p["eng"] == "tensor" and rec["eng"] == "tensor" \
                        and p["slot"] is None:
                    continue
                key = ("s", p["slot"]) if p["slot"] is not None else ("c", p["eng"])
                if key not in keep or keep[key] < d:
                    keep[key] = d
            rec["deps"] = sorted(keep.values())
            for d in rec["deps"]:
                p = self.instrs[d]
                if p["slot"] is None:
                    p["flag"] = True
        lastof = {}
        run_slots = {}
        for rec in self.instrs:
            if rec["eng"] == "__bar__":
                rec["bar_last"] = dict(lastof)
                rec["bar_slots"] = dict(run_slots)
            elif rec["slot"] is not None:
                run_slots[rec["slot"]] = rec["dmaval"]
            else:
                lastof[rec["eng"]] = rec["iid"]
        for rec in self.instrs:
            if rec["eng"] == "__bar__":
                for iid in rec["bar_last"].values():
                    self.instrs[iid]["flag"] = True
        cnt = {e: 0 for e in ENGINES}
        for rec in self.instrs:
            if rec["eng"] != "__bar__" and rec["flag"] and rec["slot"] is None:
                cnt[rec["eng"]] += 1
                rec["semval"] = cnt[rec["eng"]]

    def emit_one(self, eng_name, eng, chain, slots):
        satisfied = {}

        def wait(key, sem, val):
            if val > satisfied.get(key, 0):
                eng.wait_ge(sem, val)
                satisfied[key] = val

        for rec in self.instrs:
            if rec["eng"] == "__bar__":
                for e, iid in rec["bar_last"].items():
                    if e != eng_name:
                        p = self.instrs[iid]
                        wait(("c", e), chain[e], p["semval"])
                for s, v in rec["bar_slots"].items():
                    wait(("s", s), slots[s], v)
                continue
            if rec["eng"] != eng_name:
                continue
            for d in rec["deps"]:
                p = self.instrs[d]
                if p["slot"] is not None:
                    wait(("s", p["slot"]), slots[p["slot"]], p["dmaval"])
                else:
                    wait(("c", p["eng"]), chain[p["eng"]], p["semval"])
            ins = rec["emit"](eng)
            if rec["slot"] is not None:
                if rec["inc"] == 16:
                    ins.then_inc(slots[rec["slot"]], 16)
                else:
                    ins.then_inc(slots[rec["slot"]])
            elif rec["flag"]:
                ins.then_inc(chain[eng_name], 1)


def build_program(phases=("A", "B", "C"), dbg=False):
    nc = bass.Bass("TRN2", target_bir_lowering=False, debug=False)

    dx = nc.dram_tensor("x", [D, TPC], F32, kind="ExternalInput").ap()
    dwq = nc.dram_tensor("wqT", [D, D], F32, kind="ExternalInput").ap()
    dwk = nc.dram_tensor("wkT", [D, D], F32, kind="ExternalInput").ap()
    dwv = nc.dram_tensor("wvT", [D, D], F32, kind="ExternalInput").ap()
    dwo = nc.dram_tensor("woT", [D, D], F32, kind="ExternalInput").ap()
    dwf = nc.dram_tensor("wfT", [D, H], F32, kind="ExternalInput").ap()
    dwg = nc.dram_tensor("wgT", [D, NE], F32, kind="ExternalInput").ap()
    dw1 = nc.dram_tensor("w1", [NE, D, INTER], F32, kind="ExternalInput").ap()
    dw2 = nc.dram_tensor("w2", [NE, INTER, D], F32, kind="ExternalInput").ap()
    db1 = nc.dram_tensor("b1p", [128, 256], F32, kind="ExternalInput").ap()
    db2 = nc.dram_tensor("b2", [NE, D], F32, kind="ExternalInput").ap()
    ddelta = nc.dram_tensor("delta", [1, H], F32, kind="ExternalInput").ap()
    dnw = nc.dram_tensor("nw", [1, D], F32, kind="ExternalInput").ap()
    dtriu = nc.dram_tensor("triu", [128, 128], F32, kind="ExternalInput").ap()
    dident = nc.dram_tensor("ident", [128, 128], F32, kind="ExternalInput").ap()
    dx_tm = nc.dram_tensor("x_tm", [TPC, D], F32, kind="ExternalInput").ap()
    diota = nc.dram_tensor("iota128", [128, 1], F32, kind="ExternalInput").ap()
    decb = nc.dram_tensor("ecb", [128, NE], F32, kind="ExternalInput").ap()
    dtrius = nc.dram_tensor("trius", [128, 128], F32, kind="ExternalInput").ap()
    dmaskS = nc.dram_tensor("maskS", [128, 1], F32, kind="ExternalInput").ap()
    dmaskC = nc.dram_tensor("maskC", [128, 1], F32, kind="ExternalInput").ap()
    dout = nc.dram_tensor("out", [TPC, D], F32, kind="ExternalOutput").ap()

    kd = dict(kind="ExternalOutput") if dbg else {}
    xn_scr = nc.dram_tensor("xn_scr", [128, 8, TPC], BF16, **kd).ap()
    o_scr = nc.dram_tensor("o_scr", [NCHUNK, 128, D], F32, **kd).ap()
    qh_scr = nc.dram_tensor("qh_scr", [NCHUNK, 64, 2048], BF16, **kd).ap()
    gts_scr = nc.dram_tensor("gts_scr", [NE, TPC], F32, **kd).ap()
    xn_tm_scr = nc.dram_tensor("xn_tm_scr", [TPC, D], F32, **kd).ap()
    o_cat = nc.dram_tensor("o_cat", [NSEG, D], F32, **kd).ap()
    idx_scr = nc.dram_tensor("idx_scr", [NSEG, 1], mybir.dt.int32, **kd).ap()
    if dbg:
        dbg_g = nc.dram_tensor("dbg_g", [NCHUNK, 128, D], F32, kind="ExternalOutput").ap()
        dbg_ga1 = nc.dram_tensor("dbg_ga1", [NCHUNK, 128, D], F32, kind="ExternalOutput").ap()
        dbg_yt = nc.dram_tensor("dbg_yt", [NCHUNK, 128, D], F32, kind="ExternalOutput").ap()
        dbg_g12 = nc.dram_tensor("dbg_g12", [128, 64], F32, kind="ExternalOutput").ap()
        dbg_yt1 = nc.dram_tensor("dbg_yt1", [NCHUNK, 128, D], F32, kind="ExternalOutput").ap()
        dbg_lgs = nc.dram_tensor("dbg_lgs", [NCHUNK, 128, 8], F32, kind="ExternalOutput").ap()
        dbg_gts = nc.dram_tensor("dbg_gts", [NCHUNK, 128, 8], F32, kind="ExternalOutput").ap()
        dbg_lf = nc.dram_tensor("dbg_lf", [NCHUNK, 128, 16], F32, kind="ExternalOutput").ap()
        dbg_elb = nc.dram_tensor("dbg_elb", [NCHUNK, 128, 16], F32, kind="ExternalOutput").ap()
    cc_in = nc.dram_tensor("cc_in", [64, H * 64], F32).ap()
    cc_out = nc.dram_tensor("cc_out", [64, H * 64], F32).ap()

    dx3 = dx.rearrange("(k p) n -> p k n", p=128)

    ir = IR()
    stack = contextlib.ExitStack()

    def sbuf(name, shape, dtype):
        return nc.alloc_sbuf_tensor(name, shape, dtype).ap()

    triu = sbuf("sb_triu", [128, 128], F32)
    ident = sbuf("sb_ident", [128, 128], F32)
    maskS = sbuf("sb_maskS", [128, 1], F32)
    maskC = sbuf("sb_maskC", [128, 1], F32)
    ones_col = sbuf("ones_col", [128, 1], F32)
    ones_row = sbuf("ones_row", [1, 128], F32)
    iota_sb = sbuf("iota_sb", [128, 1], F32)
    ecb_sb = sbuf("ecb_sb", [128, NE], F32)
    trius_sb = sbuf("trius_sb", [128, 128], F32)
    g12_all = sbuf("g12_all", [128, 64], F32)
    b2_sb = sbuf("b2_sb", [8, 1024], F32)
    loc_all = sbuf("loc_all", [128, 64], mybir.dt.int32)
    base_row = sbuf("base_row", [1, NE], F32)
    zeroi = sbuf("zeroi", [128, 96], mybir.dt.int32)

    ARENA = 200 * 1024
    arena = nc.alloc_sbuf_tensor("arena", [128, ARENA], mybir.dt.uint8).ap()

    class Carver:
        def __init__(self):
            self.off = 0

        def t(self, shape, dtype):
            n = int(np.prod(shape[1:])) * mybir.dt.size(dtype)
            n = (n + 255) // 256 * 256
            o = self.off
            assert o + n <= ARENA, (o, n)
            self.off += n
            ap = arena[:, o:o + n].bitcast(dtype)
            if len(shape) == 2:
                ap = ap[0:shape[0], 0:shape[1]]
            else:
                ap = ap[:, 0:shape[1] * shape[2]]
                ap = ap.rearrange("p (a b) -> p a b", a=shape[1])
                ap = ap[0:shape[0]]
            return ap

    ca = Carver()
    xf = [ca.t([128, 8 * 512], F32) for _ in range(2)]
    wland = arena[:, 0:32768].bitcast(F32)
    xnf = ca.t([128, 8 * 512], F32)
    xnb = [ca.t([128, 8 * 512], BF16) for _ in range(2)]
    wq_b = ca.t([128, 8 * 1024], BF16)
    wk_b = ca.t([128, 8 * 1024], BF16)
    wv_b = ca.t([128, 8 * 1024], BF16)
    wf_b = ca.t([128, 8 * 16], BF16)
    wg_f = ca.t([128, 8 * 8], F32)
    q_tm = ca.t([128, 1024], F32)
    k_tm = ca.t([128, 1024], F32)
    v_tm = ca.t([128, 1024], BF16)
    qd_tm = ca.t([128, 1024], F32)
    kd_tmf = ca.t([128, 1024], F32)
    kd_tmb = ca.t([128, 1024], BF16)
    qd_fm = ca.t([64, 2048], BF16)
    kd_fm = ca.t([64, 2048], BF16)
    qh_fm = [ca.t([64, 2048], BF16) for _ in range(2)]
    a_bf4 = [ca.t([128, 512], BF16) for _ in range(2)]
    sqb = [ca.t([128, 512], F32) for _ in range(2)]
    _o_tm1 = ca.t([128, 1024], F32)
    o_tm = [_o_tm1, _o_tm1]
    lf_sb = ca.t([128, 16], F32)
    zf_sb = ca.t([128, 16], F32)
    az_sb = ca.t([128, 16], F32)
    e1f_sb = ca.t([128, 16], F32)
    l1f_sb = ca.t([128, 16], F32)
    zm_sb = ca.t([128, 16], F32)
    expLb = ca.t([128, 16], F32)
    emLb = ca.t([128, 16], F32)
    eTotD = ca.t([128, 16], F32)
    PvecD = ca.t([128, 16], F32)
    PM = ca.t([128, 16], F32)
    totR = ca.t([1, 16], F32)
    delta_b = ca.t([128, 16], F32)
    S_f = ca.t([64, 1024], F32)
    S_bf = ca.t([64, 1024], BF16)
    Scon = ca.t([64, 1024], F32)
    lgs_sb = ca.t([128, 8], F32)
    m1g = ca.t([128, 1], F32)
    m2g = ca.t([128, 1], F32)
    e1g = ca.t([128, 8], F32)
    e2g = ca.t([128, 8], F32)
    lg2 = ca.t([128, 8], F32)
    w1g = ca.t([128, 1], F32)
    w2g = ca.t([128, 1], F32)
    gts_tm = ca.t([128, 8], F32)
    gtsT_t = ca.t([8, 512], F32)
    mrow = ca.t([1, 512], F32)
    vrow = ca.t([1, 512], F32)
    m2row = ca.t([1, 512], F32)
    srow = ca.t([1, 512], F32)
    rrow = ca.t([1, 512], F32)
    brow = ca.t([1, 512], F32)
    drow = ca.t([1, 16], F32)
    f01 = ca.t([128, 8], F32)
    rr8 = ca.t([128, 8], F32)
    s8 = ca.t([128, 8], F32)
    loc1f = ca.t([128, 1], F32)
    loc2f = ca.t([128, 1], F32)
    loc_i = ca.t([128, 2], mybir.dt.int32)
    tidf = ca.t([128, 1], F32)
    tidi = ca.t([128, 1], mybir.dt.int32)
    cnt_r = ca.t([1, 8], F32)
    _xnt1 = ca.t([128, 1024], F32)
    xn_tm_t = [_xnt1, _xnt1]

    cb = Carver()
    ht = cb.t([128, 32 * 512], BF16)
    w1l = [cb.t([128, 1024], F32) for _ in range(3)]
    w1b = [cb.t([128, 1024], BF16) for _ in range(3)]
    w2l = [cb.t([128, 2048], F32) for _ in range(2)]
    w2b = [cb.t([128, 4096], BF16) for _ in range(2)]
    b1_sb = cb.t([128, 256], F32)
    xg_tm = [cb.t([128, 1024], F32) for _ in range(2)]
    xg_fm = cb.t([128, 8 * 512], BF16)
    o_fm_sb = cb.t([128, 8 * 512], F32)
    o_tm_t = [cb.t([128, 1024], F32) for _ in range(2)]
    idxt = [cb.t([128, 4], mybir.dt.int32) for _ in range(2)]

    cv = Carver()
    wo_b = cv.t([128, 8 * 1024], BF16)
    wlandc = cv.t([128, 8192], F32)
    o_ld = [cv.t([128, 1024], F32) for _ in range(2)]
    qh_ld = [cv.t([64, 2048], BF16) for _ in range(2)]
    of_sb = cv.t([128, 1024], F32)
    sqc = cv.t([128, 1024], F32)
    ssum = cv.t([128, 1], F32)
    rstd = cv.t([128, 1], F32)
    on1 = cv.t([128, 1024], F32)
    on_f = cv.t([128, 1024], F32)
    on_fm = cv.t([128, 8 * 512], BF16)
    gtsT_c = cv.t([8, 512], F32)
    ga1 = [cv.t([128, 1024], F32) for _ in range(2)]
    ga2 = [cv.t([128, 1024], F32) for _ in range(2)]
    yt = cv.t([128, 1024], F32)
    yt2 = cv.t([128, 1024], F32)
    gfin = cv.t([128, 1024], F32)
    xt_c = [cv.t([128, 1024], F32) for _ in range(2)]
    ostage_tm = [cv.t([128, 1024], F32) for _ in range(2)]
    tmpc = cv.t([128, 512], F32)
    S_pair = cv.t([64, 1024], F32)
    Scor_f = cv.t([64, 1024], F32)
    Scor_b = cv.t([64, 1024], BF16)
    nw_row = cv.t([1, 1024], F32)
    nw_bc = cv.t([128, 1024], F32)
    assert max(ca.off, cb.off, cv.off) <= ARENA, (ca.off, cb.off, cv.off)

    pb = [stack.enter_context(nc.psum_tensor("pb%d" % i, [128, 512], F32))
          for i in range(8)]
    pbt = ["pb%d" % i for i in range(8)]

    A = ir.add

    def mm(bank_ap, lhsT, rhs, start, stop, reads, wtok, tag=""):
        A("tensor",
          (lambda e, o=bank_ap, l=lhsT, r=rhs, st=start, sp=stop:
           e.matmul(o, l, r, start=st, stop=sp)),
          reads=reads, writes=(wtok,), tag=tag)

    A("vector", lambda e: e.memset(ones_col[:], 1.0), writes=("ones_col",))
    A("vector", lambda e: e.memset(ones_row[:], 1.0), writes=("ones_row",))
    A("vector", lambda e: e.memset(S_f[:], 0.0), writes=("S_f",))
    A("vector", lambda e: e.memset(S_bf[:], 0.0), writes=("S_bf",))
    A("vector", lambda e: e.memset(PvecD[:], 1.0), writes=("PvecD",))
    A("sync", lambda e: e.dma_start(out=triu[:], in_=dtriu), writes=("triu",), slot="m0")
    A("sync", lambda e: e.dma_start(out=ident[:], in_=dident), writes=("ident",), slot="m1")
    A("sync", lambda e: e.dma_start(out=maskS[:], in_=dmaskS), writes=("maskS",), slot="m2")
    A("sync", lambda e: e.dma_start(out=maskC[:], in_=dmaskC), writes=("maskC",), slot="m3")
    A("sync", lambda e: e.dma_start(out=drow[:], in_=ddelta), writes=("drow",), slot="m4")
    A("sync", lambda e: e.dma_start(
        out=wg_f.rearrange("p (k c) -> p k c", k=8),
        in_=dwg.rearrange("(k p) c -> p k c", p=128)), writes=("wg_f",), slot="m5")
    for wdst, wtok, src in ((wq_b, "wq_b", dwq), (wk_b, "wk_b", dwk),
                            (wv_b, "wv_b", dwv)):
        A("sync", (lambda e, s=src: e.dma_start(
            out=wland.rearrange("p (k c) -> p k c", k=8),
            in_=s.rearrange("(k p) c -> p k c", p=128))),
          writes=("wland",), slot="wld")
        A("scalar", (lambda e, d=wdst: e.copy(out=d, in_=wland[:])),
          reads=("wland",), writes=(wtok,))
    A("sync", lambda e: e.dma_start(
        out=wland.rearrange("p (k c) -> p k c", k=8)[:, :, 0:16],
        in_=dwf.rearrange("(k p) c -> p k c", p=128)),
      writes=("wland",), slot="wld")
    A("scalar", lambda e: e.copy(
        out=wf_b.rearrange("p (k c) -> p k c", k=8),
        in_=wland.rearrange("p (k c) -> p k c", k=8)[:, :, 0:16]),
      reads=("wland",), writes=("wf_b",))
    A("sync", lambda e: e.dma_start(out=iota_sb[:], in_=diota), writes=("iota_sb",), slot="m1")
    A("sync", lambda e: e.dma_start(out=ecb_sb[:], in_=decb), writes=("ecb_sb",), slot="m2")
    A("sync", lambda e: e.dma_start(out=trius_sb[:], in_=dtrius), writes=("trius_sb",), slot="m3")
    A("vector", lambda e: e.memset(base_row[:], 0.0), writes=("base_row",))
    A("vector", lambda e: e.memset(zeroi[:], 0), writes=("zeroi",))
    A("pool", lambda e: e.dma_start(
        out=idx_scr.rearrange("(p a) b -> p (a b)", p=128), in_=zeroi[:]),
      reads=("zeroi",), writes=("idx_scr",), slot="isc")
    mm(pb[2][0:128, 0:16], ones_row[:], drow[:], True, True,
       ("ones_row", "drow"), pbt[2], "deltab")
    A("vector", lambda e: e.tensor_copy(out=delta_b[:], in_=pb[2][0:128, 0:16]),
      reads=(pbt[2],), writes=("delta_b",))

    ir.barrier()

    for t in range(NTILE if "A" in phases else 0):
        xb = t % 2
        xtok = "xf%d" % xb
        xntok = "xnb%d" % xb
        A("sync", (lambda e, t=t, xb=xb: e.dma_start(
            out=xf[xb].rearrange("p (k n) -> p k n", k=8),
            in_=dx3[:, :, t * 512:(t + 1) * 512])),
          writes=(xtok,), slot=xtok)
        for k in range(8):
            mm(pb[6][0:1, 0:512], ones_col[:],
               xf[xb][:, k * 512:(k + 1) * 512],
               k == 0, k == 7, ("ones_col", xtok), pbt[6], "lnsum")
        for k in range(8):
            sq = sqb[k % 2]
            sqt = "sqb%d" % (k % 2)
            A("scalar", (lambda e, k=k, xb=xb, sq=sq: e.activation(
                out=sq[:], in_=xf[xb][:, k * 512:(k + 1) * 512], func=AF.Square)),
              reads=(xtok,), writes=(sqt,))
            mm(pb[7][0:1, 0:512], ones_col[:], sq[:], k == 0, k == 7,
               ("ones_col", sqt), pbt[7], "lnsq")
        A("vector", lambda e: e.tensor_scalar_mul(mrow[:], pb[6][0:1, 0:512], 1.0 / 1024.0),
          reads=(pbt[6],), writes=("mrow",))
        A("vector", lambda e: e.tensor_scalar_mul(vrow[:], pb[7][0:1, 0:512], 1.0 / 1024.0),
          reads=(pbt[7],), writes=("vrow",))
        A("vector", lambda e: e.tensor_tensor(out=m2row[:], in0=mrow[:], in1=mrow[:], op=ALU.mult),
          reads=("mrow",), writes=("m2row",))
        A("vector", lambda e: e.tensor_sub(vrow[:], vrow[:], m2row[:]),
          reads=("vrow", "m2row"), writes=("vrow",))
        A("vector", lambda e: e.tensor_scalar_add(vrow[:], vrow[:], 1e-5),
          reads=("vrow",), writes=("vrow",))
        A("scalar", lambda e: e.activation(out=srow[:], in_=vrow[:], func=AF.Sqrt),
          reads=("vrow",), writes=("srow",))
        A("vector", lambda e: e.reciprocal(out=rrow[:], in_=srow[:]),
          reads=("srow",), writes=("rrow",))
        A("vector", lambda e: e.tensor_tensor(out=brow[:], in0=mrow[:], in1=rrow[:], op=ALU.mult),
          reads=("mrow", "rrow"), writes=("brow",))
        A("vector", lambda e: e.tensor_scalar_mul(brow[:], brow[:], -1.0),
          reads=("brow",), writes=("brow",))
        mm(pb[2][:, 0:512], ones_row[:], rrow[:], True, True,
           ("ones_row", "rrow"), pbt[2], "bca")
        mm(pb[3][:, 0:512], ones_row[:], brow[:], True, True,
           ("ones_row", "brow"), pbt[3], "bcb")
        def _b8(big, small):
            return bass.broadcast_tensor_aps(
                big.rearrange("p (u n) -> p u n", u=8),
                small.rearrange("p (u n) -> p u n", u=1))
        A("vector", (lambda e, xb=xb: e.tensor_tensor(
            out=xnf.rearrange("p (u n) -> p u n", u=8),
            in0=_b8(xf[xb], pb[2][:, 0:512])[0],
            in1=_b8(xf[xb], pb[2][:, 0:512])[1], op=ALU.mult)),
          reads=(xtok, pbt[2]), writes=("xnf",))
        A("vector", (lambda e: e.tensor_tensor(
            out=xnf.rearrange("p (u n) -> p u n", u=8),
            in0=xnf.rearrange("p (u n) -> p u n", u=8),
            in1=_b8(xnf, pb[3][:, 0:512])[1], op=ALU.add)),
          reads=("xnf", pbt[3]), writes=("xnf",))
        A("scalar", (lambda e, xb=xb: e.copy(out=xnb[xb][:], in_=xnf[:])),
          reads=("xnf",), writes=(xntok,))
        A("pool", (lambda e, t=t, xb=xb: e.dma_start(
            out=xn_scr[:, :, t * 512:(t + 1) * 512],
            in_=xnb[xb].rearrange("p (k n) -> p k n", k=8))),
          reads=(xntok,), slot="xns%d" % xb)

        for cc_i in range(4):
            c = t * 4 + cc_i

            def xsl(k, cc_i=cc_i, xb=xb):
                return xnb[xb][:, k * 512 + cc_i * 128: k * 512 + cc_i * 128 + 128]

            def xslf(k, cc_i=cc_i):
                return xnf[:, k * 512 + cc_i * 128: k * 512 + cc_i * 128 + 128]

            for k in range(8):
                mm(pb[6][0:128, 0:16], xsl(k), wf_b[:, k * 16:(k + 1) * 16],
                   k == 0, k == 7, (xntok, "wf_b"), pbt[6], "f")
                mm(pb[7][0:128, 0:8], xslf(k), wg_f[:, k * 8:(k + 1) * 8],
                   k == 0, k == 7, ("xnf", "wg_f"), pbt[7], "lg")
            A("vector", lambda e: e.tensor_tensor(out=zf_sb[:], in0=pb[6][0:128, 0:16],
                                                  in1=delta_b[:], op=ALU.add),
              reads=(pbt[6], "delta_b"), writes=("zf",))
            A("scalar", lambda e: e.activation(out=az_sb[:], in_=zf_sb[:], func=AF.Abs),
              reads=("zf",), writes=("az",))
            A("scalar", lambda e: e.activation(out=e1f_sb[:], in_=az_sb[:], func=AF.Exp, scale=-1.0),
              reads=("az",), writes=("e1f",))
            A("scalar", lambda e: e.activation(out=l1f_sb[:], in_=e1f_sb[:], func=AF.Ln, bias=1.0),
              reads=("e1f",), writes=("l1f",))
            A("vector", lambda e: e.tensor_single_scalar(out=zm_sb[:], in_=zf_sb[:],
                                                       scalar=0.0, op=ALU.min),
              reads=("zf",), writes=("zm",))
            A("pool", lambda e: e.tensor_sub(lf_sb[:], zm_sb[:], l1f_sb[:]),
              reads=("zm", "l1f"), writes=("lf",))
            A("vector", lambda e: e.tensor_copy(out=lgs_sb[:], in_=pb[7][0:128, 0:8]),
              reads=(pbt[7],), writes=("lgs",))
            A("vector", lambda e: e.reduce_max(out=m1g[:], in_=lgs_sb[:], axis=AX.X),
              reads=("lgs",), writes=("m1g",))
            A("vector", lambda e: e.tensor_single_scalar(out=e1g[:], in_=lgs_sb[:],
                                                       scalar=m1g[:, 0:1], op=ALU.is_ge),
              reads=("lgs", "m1g"), writes=("e1g",))
            A("vector", lambda e: e.scalar_tensor_tensor(out=lg2[:], in0=e1g[:], scalar=-1e30,
                                                       in1=lgs_sb[:], op0=ALU.mult, op1=ALU.add),
              reads=("e1g", "lgs"), writes=("lg2",))
            A("vector", lambda e: e.reduce_max(out=m2g[:], in_=lg2[:], axis=AX.X),
              reads=("lg2",), writes=("m2g",))
            A("vector", lambda e: e.tensor_single_scalar(out=e2g[:], in_=lg2[:],
                                                       scalar=m2g[:, 0:1], op=ALU.is_ge),
              reads=("lg2", "m2g"), writes=("e2g",))
            A("scalar", lambda e: e.activation(out=w1g[:], in_=m2g[:], func=AF.Sigmoid,
                                               bias=m1g[:, 0:1], scale=-1.0),
              reads=("m2g", "m1g"), writes=("w1g",))
            A("vector", lambda e: e.tensor_scalar(out=w2g[:], in0=w1g[:], scalar1=-1.0,
                                                scalar2=1.0, op0=ALU.mult, op1=ALU.add),
              reads=("w1g",), writes=("w2g",))
            A("vector", lambda e: e.tensor_scalar_mul(gts_tm[:], e1g[:], w1g[:, 0:1]),
              reads=("e1g", "w1g"), writes=("gts_tm",))
            A("vector", lambda e: e.scalar_tensor_tensor(out=gts_tm[:], in0=e2g[:],
                                                       scalar=w2g[:, 0:1], in1=gts_tm[:],
                                                       op0=ALU.mult, op1=ALU.add),
              reads=("e2g", "w2g", "gts_tm"), writes=("gts_tm",))
            if dbg:
                A("pool", (lambda e, c=c: e.dma_start(out=dbg_lgs[c], in_=lgs_sb[:])),
                  reads=("lgs",), slot="dbg0")
                A("pool", (lambda e, c=c: e.dma_start(out=dbg_gts[c], in_=gts_tm[:])),
                  reads=("gts_tm",), slot="dbg1")
                A("pool", (lambda e, c=c: e.dma_start(out=dbg_lf[c], in_=lf_sb[:])),
                  reads=("lf",), slot="dbg2")
                A("pool", (lambda e, c=c: e.dma_start(out=dbg_elb[c], in_=expLb[:])),
                  reads=("expLb",), slot="dbg3")
            for k in range(8):
                mm(pb[0][:], xsl(k), wq_b[:, k * 1024:k * 1024 + 512],
                   k == 0, k == 7, (xntok, "wq_b"), pbt[0], "q0")
                mm(pb[1][:], xsl(k), wq_b[:, k * 1024 + 512:(k + 1) * 1024],
                   k == 0, k == 7, (xntok, "wq_b"), pbt[1], "q1")
            A("scalar", lambda e: e.activation(out=q_tm[:, 0:512], in_=pb[0][:], func=AF.Silu),
              reads=(pbt[0],), writes=("q_tm",))
            A("scalar", lambda e: e.activation(out=q_tm[:, 512:1024], in_=pb[1][:], func=AF.Silu),
              reads=(pbt[1],), writes=("q_tm",))
            for k in range(8):
                mm(pb[2][:], xsl(k), wk_b[:, k * 1024:k * 1024 + 512],
                   k == 0, k == 7, (xntok, "wk_b"), pbt[2], "k0")
                mm(pb[3][:], xsl(k), wk_b[:, k * 1024 + 512:(k + 1) * 1024],
                   k == 0, k == 7, (xntok, "wk_b"), pbt[3], "k1")
            A("scalar", lambda e: e.activation(out=k_tm[:, 0:512], in_=pb[2][:], func=AF.Silu),
              reads=(pbt[2],), writes=("k_tm",))
            A("scalar", lambda e: e.activation(out=k_tm[:, 512:1024], in_=pb[3][:], func=AF.Silu),
              reads=(pbt[3],), writes=("k_tm",))
            for k in range(8):
                mm(pb[0][:], xsl(k), wv_b[:, k * 1024:k * 1024 + 512],
                   k == 0, k == 7, (xntok, "wv_b"), pbt[0], "v0")
                mm(pb[1][:], xsl(k), wv_b[:, k * 1024 + 512:(k + 1) * 1024],
                   k == 0, k == 7, (xntok, "wv_b"), pbt[1], "v1")
            A("scalar", lambda e: e.activation(out=v_tm[:, 0:512], in_=pb[0][:], func=AF.Copy),
              reads=(pbt[0],), writes=("v_tm",))
            A("scalar", lambda e: e.activation(out=v_tm[:, 512:1024], in_=pb[1][:], func=AF.Copy),
              reads=(pbt[1],), writes=("v_tm",))
            mm(pb[6][0:128, 0:16], triu[:], lf_sb[:], True, True,
               ("triu", "lf"), pbt[6], "cumsum")
            A("scalar", lambda e: e.activation(out=expLb[:], in_=pb[6][0:128, 0:16], func=AF.Exp),
              reads=(pbt[6],), writes=("expLb",))
            A("scalar", lambda e: e.activation(out=emLb[:], in_=pb[6][0:128, 0:16],
                                               func=AF.Exp, scale=-1.0),
              reads=(pbt[6],), writes=("emLb",))
            mm(pb[7][0:1, 0:16], ones_col[:], lf_sb[:], True, True,
               ("ones_col", "lf"), pbt[7], "tot")
            A("vector", lambda e: e.tensor_copy(out=totR[:], in_=pb[7][0:1, 0:16]),
              reads=(pbt[7],), writes=("totR",))
            mm(pb[7][0:128, 0:16], ones_row[:], totR[:], True, True,
               ("ones_row", "totR"), pbt[7], "totb")
            A("scalar", lambda e: e.activation(out=eTotD[:], in_=pb[7][0:128, 0:16], func=AF.Exp),
              reads=(pbt[7],), writes=("eTotD",))
            def _bh(big, small, hh=16):
                return bass.broadcast_tensor_aps(
                    big.rearrange("p (h n) -> p h n", h=hh),
                    small.rearrange("p (h u) -> p h u", u=1))
            A("vector", (lambda e: e.tensor_tensor(
                out=qd_tm.rearrange("p (h n) -> p h n", h=16),
                in0=_bh(q_tm, expLb)[0], in1=_bh(q_tm, expLb)[1], op=ALU.mult)),
              reads=("q_tm", "expLb"), writes=("qd_tm",))
            A("vector", (lambda e: e.tensor_tensor(
                out=kd_tmf.rearrange("p (h n) -> p h n", h=16),
                in0=_bh(k_tm, emLb)[0], in1=_bh(k_tm, emLb)[1], op=ALU.mult)),
              reads=("k_tm", "emLb"), writes=("kd_tmf",))
            A("vector", (lambda e: e.tensor_tensor(
                out=kd_tmb.rearrange("p (h n) -> p h n", h=16),
                in0=_bh(k_tm, emLb)[0], in1=_bh(k_tm, emLb)[1], op=ALU.mult)),
              reads=("k_tm", "emLb"), writes=("kd_tmb",))
            for grp in range(4):
                bank = 2 + (grp % 2)
                for j in range(4):
                    h = grp * 4 + j
                    A("tensor", (lambda e, h=h, j=j, bank=bank: e.transpose(
                        pb[bank][0:64, j * 128:(j + 1) * 128],
                        qd_tm[:, h * 64:(h + 1) * 64], ident[:])),
                      reads=("qd_tm", "ident"), writes=(pbt[bank],))
                A("scalar", (lambda e, grp=grp, bank=bank: e.copy(
                    out=qd_fm[:, grp * 512:(grp + 1) * 512], in_=pb[bank][0:64, 0:512])),
                  reads=(pbt[bank],), writes=("qd_fm",))
            for grp in range(4):
                bank = 2 + (grp % 2)
                for j in range(4):
                    h = grp * 4 + j
                    A("tensor", (lambda e, h=h, j=j, bank=bank: e.transpose(
                        pb[bank][0:64, j * 128:(j + 1) * 128],
                        kd_tmf[:, h * 64:(h + 1) * 64], ident[:])),
                      reads=("kd_tmf", "ident"), writes=(pbt[bank],))
                A("vector", (lambda e, grp=grp, bank=bank: e.tensor_copy(
                    out=kd_fm[:, grp * 512:(grp + 1) * 512], in_=pb[bank][0:64, 0:512])),
                  reads=(pbt[bank],), writes=("kd_fm",))
            qhb = c % 2
            A("vector", lambda e: e.tensor_scalar_mul(PM[:], PvecD[:], maskS[:, 0:1]),
              reads=("PvecD", "maskS"), writes=("PM",))
            A("vector", (lambda e, qhb=qhb: e.tensor_tensor(
                out=qh_fm[qhb].rearrange("p (h n) -> p h n", h=16),
                in0=_bh(qd_fm, PM[0:64, :])[0],
                in1=_bh(qd_fm, PM[0:64, :])[1], op=ALU.mult)),
              reads=("qd_fm", "PM"), writes=("qh_fm%d" % qhb,))
            A("pool", (lambda e, c=c, qhb=qhb: e.dma_start(
                out=qh_scr[c], in_=qh_fm[qhb][:])),
              reads=("qh_fm%d" % qhb,), slot="qs%d" % qhb)
            ob = c % 2
            for grp in range(4):
                abank = 2 + (grp % 2)
                aslot = grp % 2
                for j in range(4):
                    h = grp * 4 + j
                    mm(pb[abank][:, j * 128:(j + 1) * 128],
                       kd_fm[:, h * 128:(h + 1) * 128],
                       qd_fm[:, h * 128:(h + 1) * 128],
                       True, True, ("kd_fm", "qd_fm"), pbt[abank], "amm")
                A("vector", (lambda e, abank=abank, aslot=aslot: e.tensor_tensor(
                    out=a_bf4[aslot].rearrange("p (u n) -> p u n", u=4),
                    in0=pb[abank][:, 0:512].rearrange("p (u n) -> p u n", u=4),
                    in1=bass.broadcast_tensor_aps(
                        pb[abank][:, 0:512].rearrange("p (u n) -> p u n", u=4),
                        triu.rearrange("p (u n) -> p u n", u=1))[1],
                    op=ALU.mult)),
                  reads=(pbt[abank], "triu"), writes=("a_bf%d" % aslot,))
                for j in range(4):
                    h = grp * 4 + j
                    obank = 4 + (h // 8)
                    ocol = (h % 8) * 64
                    ubank = 6 + (h // 8)
                    mm(pb[obank][:, ocol:ocol + 64],
                       a_bf4[aslot][:, j * 128:(j + 1) * 128],
                       v_tm[:, h * 64:(h + 1) * 64], True, False,
                       ("a_bf%d" % aslot, "v_tm"), pbt[obank], "ointra")
                    mm(pb[obank][:, ocol:ocol + 64],
                       qd_fm[:, h * 128:(h + 1) * 128],
                       S_bf[:, h * 64:(h + 1) * 64],
                       False, True, ("qd_fm", "S_bf"), pbt[obank], "ointer")
                    mm(pb[ubank][0:64, (h % 8) * 64:(h % 8) * 64 + 64],
                       kd_tmb[:, h * 64:(h + 1) * 64],
                       v_tm[:, h * 64:(h + 1) * 64], True, True,
                       ("kd_tmb", "v_tm"), pbt[ubank], "umm")
            A("vector", lambda e: e.tensor_tensor(
                out=S_f[:, 0:512], in0=S_f[:, 0:512], in1=pb[6][0:64, 0:512],
                op=ALU.add),
              reads=("S_f", pbt[6]), writes=("S_f",))
            A("vector", lambda e: e.tensor_tensor(
                out=S_f[:, 512:1024], in0=S_f[:, 512:1024], in1=pb[7][0:64, 0:512],
                op=ALU.add),
              reads=("S_f", pbt[7]), writes=("S_f",))
            A("vector", lambda e: e.tensor_tensor(
                out=S_f.rearrange("p (h n) -> p h n", h=16),
                in0=_bh(S_f, eTotD[0:64, :])[0],
                in1=_bh(S_f, eTotD[0:64, :])[1], op=ALU.mult),
              reads=("S_f", "eTotD"), writes=("S_f",))
            A("vector", (lambda e, ob=ob: e.tensor_copy(out=o_tm[ob][:, 0:512], in_=pb[4][:])),
              reads=(pbt[4],), writes=("o_tm0",))
            A("vector", (lambda e, ob=ob: e.tensor_copy(out=o_tm[ob][:, 512:1024], in_=pb[5][:])),
              reads=(pbt[5],), writes=("o_tm0",))
            A("pool", (lambda e, c=c, ob=ob: e.dma_start(out=o_scr[c], in_=o_tm[ob][:])),
              reads=("o_tm0",), slot="os0")
            A("vector", lambda e: e.tensor_copy(out=S_bf[:], in_=S_f[:]),
              reads=("S_f",), writes=("S_bf",))
            A("vector", lambda e: e.tensor_tensor(out=PvecD[:], in0=PvecD[:],
                                                  in1=eTotD[:], op=ALU.mult),
              reads=("PvecD", "eTotD"), writes=("PvecD",))
            A("tensor", lambda e: e.transpose(pb[2][0:8, 0:128], gts_tm[:], ident[:]),
              reads=("gts_tm", "ident"), writes=(pbt[2],))
            A("vector", (lambda e, cc_i=cc_i: e.tensor_copy(
                out=gtsT_t[:, cc_i * 128:(cc_i + 1) * 128], in_=pb[2][0:8, 0:128])),
              reads=(pbt[2],), writes=("gtsT_t",))
            A("vector", lambda e: e.tensor_tensor(out=f01[:], in0=e1g[:], in1=e2g[:],
                                                  op=ALU.add),
              reads=("e1g", "e2g"), writes=("f01",))
            mm(pb[3][0:1, 8:16], ones_col[:], f01[:], True, True,
               ("ones_col", "f01"), pbt[3], "cnt")
            mm(pb[3][0:128, 0:8], trius_sb[:], f01[:], True, False,
               ("trius_sb", "f01"), pbt[3], "pfx")
            mm(pb[3][0:128, 0:8], ones_row[:], base_row[:], False, True,
               ("ones_row", "base_row"), pbt[3], "pfxb")
            A("vector", lambda e: e.tensor_copy(out=rr8[:], in_=pb[3][0:128, 0:8]),
              reads=(pbt[3],), writes=("rr8",))
            A("vector", lambda e: e.tensor_copy(out=cnt_r[:], in_=pb[3][0:1, 8:16]),
              reads=(pbt[3],), writes=("cnt_r",))
            A("vector", lambda e: e.tensor_tensor(out=base_row[:], in0=base_row[:],
                                                  in1=cnt_r[:], op=ALU.add),
              reads=("base_row", "cnt_r"), writes=("base_row",))
            A("vector", lambda e: e.tensor_tensor(out=s8[:], in0=rr8[:], in1=ecb_sb[:],
                                                  op=ALU.add),
              reads=("rr8", "ecb_sb"), writes=("s8",))
            A("vector", lambda e: e.tensor_tensor(out=rr8[:], in0=s8[:], in1=e1g[:],
                                                  op=ALU.mult),
              reads=("s8", "e1g"), writes=("rr8",))
            A("vector", lambda e: e.reduce_sum(out=loc1f[:], in_=rr8[:], axis=AX.X),
              reads=("rr8",), writes=("loc1f",))
            A("vector", lambda e: e.tensor_tensor(out=rr8[:], in0=s8[:], in1=e2g[:],
                                                  op=ALU.mult),
              reads=("s8", "e2g"), writes=("rr8",))
            A("vector", lambda e: e.reduce_sum(out=loc2f[:], in_=rr8[:], axis=AX.X),
              reads=("rr8",), writes=("loc2f",))
            A("vector", (lambda e, cc_i=cc_i, c=c: e.tensor_copy(
                out=loc_all[:, c:c + 1], in_=loc1f[:])),
              reads=("loc1f",), writes=("loc_all",))
            A("vector", (lambda e, c=c: e.tensor_copy(
                out=loc_all[:, 32 + c:33 + c], in_=loc2f[:])),
              reads=("loc2f",), writes=("loc_all",))
            A("vector", (lambda e, c=c: e.tensor_single_scalar(
                out=tidf[:], in_=iota_sb[:], scalar=float(c * 128), op=ALU.add)),
              reads=("iota_sb",), writes=("tidf",))
            A("vector", lambda e: e.tensor_copy(out=tidi[:], in_=tidf[:]),
              reads=("tidf",), writes=("tidi",))
            A("vector", lambda e: e.tensor_tensor(out=rr8[:], in0=gts_tm[:], in1=e1g[:],
                                                  op=ALU.mult),
              reads=("gts_tm", "e1g"), writes=("rr8",))
            A("vector", (lambda e, c=c: e.reduce_sum(
                out=g12_all[:, c:c + 1], in_=rr8[:], axis=AX.X)),
              reads=("rr8",), writes=("g12_all",))
            A("vector", lambda e: e.tensor_tensor(out=rr8[:], in0=gts_tm[:], in1=e2g[:],
                                                  op=ALU.mult),
              reads=("gts_tm", "e2g"), writes=("rr8",))
            A("vector", (lambda e, c=c: e.reduce_sum(
                out=g12_all[:, 32 + c:33 + c], in_=rr8[:], axis=AX.X)),
              reads=("rr8",), writes=("g12_all",))
            A("pool", (lambda e, c=c: e.indirect_dma_start(
                out=idx_scr, out_offset=bass.IndirectOffsetOnAxis(loc_all[:, c:c + 1], 0),
                in_=tidi[:], in_offset=None)),
              reads=("tidi", "loc_all", "idx_scr"), writes=("idx_scr",), slot="isc")
            A("pool", (lambda e, c=c: e.indirect_dma_start(
                out=idx_scr, out_offset=bass.IndirectOffsetOnAxis(loc_all[:, 32 + c:33 + c], 0),
                in_=tidi[:], in_offset=None)),
              reads=("tidi", "loc_all", "idx_scr"), writes=("idx_scr",), slot="isc")
            xtp = c % 2
            for k in range(8):
                bank = 4 + (k // 4)
                A("tensor", (lambda e, k=k, cc_i=cc_i, bank=bank: e.transpose(
                    pb[bank][:, (k % 4) * 128:(k % 4) * 128 + 128],
                    xnf[:, k * 512 + cc_i * 128: k * 512 + cc_i * 128 + 128], ident[:])),
                  reads=("xnf", "ident"), writes=(pbt[bank],))
            A("vector", (lambda e, xtp=xtp: e.tensor_copy(
                out=xn_tm_t[xtp][:, 0:512], in_=pb[4][:, 0:512])),
              reads=(pbt[4],), writes=("xn_tm_t0",))
            A("vector", (lambda e, xtp=xtp: e.tensor_copy(
                out=xn_tm_t[xtp][:, 512:1024], in_=pb[5][:, 0:512])),
              reads=(pbt[5],), writes=("xn_tm_t0",))
            A("pool", (lambda e, c=c, xtp=xtp: e.dma_start(
                out=xn_tm_scr[c * 128:(c + 1) * 128, :], in_=xn_tm_t[xtp][:])),
              reads=("xn_tm_t0",), slot="xnt0")
        A("pool", (lambda e, t=t: e.dma_start(
            out=gts_scr[:, t * 512:(t + 1) * 512], in_=gtsT_t[:])),
          reads=("gtsT_t",), slot="gts")

    if "A" not in phases:
        phases = phases
    A("vector", lambda e: e.tensor_scalar_mul(Scon[:], S_f[:], maskC[0:64, 0:1]),
      reads=("S_f", "maskC"), writes=("Scon",))
    A("pool", lambda e: e.dma_start(out=cc_in, in_=Scon[:]),
      reads=("Scon",), writes=("cc_in",), slot="ccin")
    A("pool", lambda e: e.collective_compute(
        "AllReduce", ALU.add, replica_groups=[[0, 1], [2, 3], [4, 5], [6, 7]],
        ins=[cc_in.opt()], outs=[cc_out.opt()]),
      reads=("cc_in",), writes=("cc_out",), slot="cc", inc=1)

    ir.barrier()

    A("sync", lambda e: e.dma_start(out=b1_sb[:], in_=db1), writes=("b1_sb",), slot="m0")
    A("sync", lambda e: e.dma_start(out=b2_sb[:], in_=db2), writes=("b2_sb",), slot="m1")

    for e_i in range(NE if "B" in phases else 0):
        for blk in range(3):
            seg = e_i * CAP + blk * 512
            ib = blk % 2
            A("sync", (lambda e, seg=seg, ib=ib: e.dma_start(
                out=idxt[ib][:],
                in_=idx_scr[seg:seg + 512].rearrange("(p a) b -> p (a b)", p=128))),
              writes=("idxt%d" % ib,), slot="idxl%d" % ib)
            for j in range(4):
                gb_ = j % 2
                A("pool", (lambda e, ib=ib, j=j, gb_=gb_: e.indirect_dma_start(
                    out=xg_tm[gb_][:], out_offset=None, in_=xn_tm_scr,
                    in_offset=bass.IndirectOffsetOnAxis(idxt[ib][:, j:j + 1], 0))),
                  reads=("idxt%d" % ib, "xn_tm_scr"), writes=("xg_tm%d" % gb_,),
                  slot="xgt%d" % gb_)
                for k in range(8):
                    bank = 6 + (k // 4)
                    A("tensor", (lambda e, k=k, gb_=gb_, bank=bank: e.transpose(
                        pb[bank][:, (k % 4) * 128:(k % 4) * 128 + 128],
                        xg_tm[gb_][:, k * 128:(k + 1) * 128], ident[:])),
                      reads=("xg_tm%d" % gb_, "ident"), writes=(pbt[bank],))
                A("scalar", (lambda e, j=j: e.copy(
                    out=xg_fm.rearrange("p (k n) -> p k n", k=8)
                    [:, 0:4, j * 128:(j + 1) * 128],
                    in_=pb[6][:, 0:512].rearrange("p (k n) -> p k n", k=4))),
                  reads=(pbt[6],), writes=("xg_fm",))
                A("scalar", (lambda e, j=j: e.copy(
                    out=xg_fm.rearrange("p (k n) -> p k n", k=8)
                    [:, 4:8, j * 128:(j + 1) * 128],
                    in_=pb[7][:, 0:512].rearrange("p (k n) -> p k n", k=4))),
                  reads=(pbt[7],), writes=("xg_fm",))
            for m in range(32):
                ws = m % 3
                A("sync", (lambda e, e_i=e_i, m=m, ws=ws: e.dma_start(
                    out=w1l[ws].rearrange("p (k c) -> p k c", k=8),
                    in_=dw1[e_i].rearrange("(k p) c -> p k c", p=128)
                    [:, :, m * 128:(m + 1) * 128])),
                  writes=("w1l%d" % ws,), slot="w1l%d" % ws)
                A("vector", (lambda e, ws=ws: e.tensor_copy(out=w1b[ws], in_=w1l[ws][:])),
                  reads=("w1l%d" % ws,), writes=("w1b%d" % ws,))
                bH = m % 2
                for k in range(8):
                    mm(pb[bH][:], w1b[ws][:, k * 128:(k + 1) * 128],
                       xg_fm[:, k * 512:(k + 1) * 512], k == 0, k == 7,
                       ("w1b%d" % ws, "xg_fm"), pbt[bH], "h0")
                bias = b1_sb[:, e_i * 32 + m: e_i * 32 + m + 1]
                A("scalar", (lambda e, m=m, bH=bH, bias=bias: e.activation(
                    out=ht[:, m * 512:(m + 1) * 512], in_=pb[bH][:],
                    func=AF.Gelu, bias=bias)),
                  reads=(pbt[bH], "b1_sb"), writes=("ht",))
            for hc in range(8):
                w2s = hc % 2
                for hf in range(2):
                    A("sync", (lambda e, e_i=e_i, hc=hc, hf=hf, w2s=w2s: e.dma_start(
                        out=w2l[w2s].rearrange("p (k c) -> p k c", k=16),
                        in_=dw2[e_i].rearrange("(m p) c -> p m c", p=128)
                        [:, hf * 16:(hf + 1) * 16, hc * 128:(hc + 1) * 128])),
                      writes=("w2l%d" % w2s,), slot="w2l%d" % w2s)
                    A("scalar", (lambda e, hf=hf, w2s=w2s: e.copy(
                        out=w2b[w2s][:, hf * 2048:(hf + 1) * 2048], in_=w2l[w2s][:])),
                      reads=("w2l%d" % w2s,), writes=("w2b%d" % w2s,))
                bO = 2 + hc % 2
                for m in range(32):
                    mm(pb[bO][:], w2b[w2s][:, m * 128:(m + 1) * 128],
                       ht[:, m * 512:(m + 1) * 512], m == 0, m == 31,
                       ("w2b%d" % w2s, "ht"), pbt[bO], "o0")
                A("vector", (lambda e, hc=hc, bO=bO: e.tensor_copy(
                    out=o_fm_sb[:, hc * 512:(hc + 1) * 512], in_=pb[bO][:])),
                  reads=(pbt[bO],), writes=("o_fm_sb",))
            for j in range(4):
                otp = j % 2
                for k in range(8):
                    bank = 4 + (k // 4)
                    A("tensor", (lambda e, k=k, j=j, bank=bank: e.transpose(
                        pb[bank][:, (k % 4) * 128:(k % 4) * 128 + 128],
                        o_fm_sb[:, k * 512 + j * 128: k * 512 + j * 128 + 128],
                        ident[:])),
                      reads=("o_fm_sb", "ident"), writes=(pbt[bank],))
                A("vector", (lambda e, otp=otp: e.tensor_copy(
                    out=o_tm_t[otp][:, 0:512], in_=pb[4][:, 0:512])),
                  reads=(pbt[4],), writes=("o_tm_t%d" % otp,))
                A("vector", (lambda e, otp=otp: e.tensor_copy(
                    out=o_tm_t[otp][:, 512:1024], in_=pb[5][:, 0:512])),
                  reads=(pbt[5],), writes=("o_tm_t%d" % otp,))
                A("pool", (lambda e, seg=seg, j=j, otp=otp: e.dma_start(
                    out=o_cat[seg: seg + 512, :].rearrange("(p a) c -> p a c", a=4)
                    [:, j:j + 1, :],
                    in_=o_tm_t[otp][:].rearrange("p (a c) -> p a c", a=1))),
                  reads=("o_tm_t%d" % otp,), writes=("o_cat",), slot="ocs%d" % otp)

    ir.barrier()

    A("sync", lambda e: e.dma_start(
        out=wlandc.rearrange("p (k c) -> p k c", k=8),
        in_=dwo.rearrange("(k p) c -> p k c", p=128)),
      writes=("wlandc",), slot="wld")
    A("scalar", lambda e: e.copy(out=wo_b, in_=wlandc[:]),
      reads=("wlandc",), writes=("wo_b",))
    A("sync", lambda e: e.dma_start(out=S_pair[:], in_=cc_out),
      reads=("cc_out",), writes=("S_pair",), slot="sld")
    if dbg:
        A("pool", lambda e: e.dma_start(out=dbg_g12, in_=g12_all[:]),
          reads=("g12_all",), slot="dbg12")
    A("vector", lambda e: e.tensor_scalar_mul(Scor_f[:], S_pair[:], maskS[0:64, 0:1]),
      reads=("S_pair", "maskS"), writes=("Scor_f",))
    A("vector", lambda e: e.tensor_copy(out=Scor_b[:], in_=Scor_f[:]),
      reads=("Scor_f",), writes=("Scor_b",))
    A("sync", lambda e: e.dma_start(out=nw_row[:], in_=dnw),
      writes=("nw_row",), slot="m3")
    mm(pb[2][:], ones_row[:], nw_row[:, 0:512], True, True,
       ("ones_row", "nw_row"), pbt[2], "nw0")
    mm(pb[3][:], ones_row[:], nw_row[:, 512:1024], True, True,
       ("ones_row", "nw_row"), pbt[3], "nw1")
    A("vector", lambda e: e.tensor_copy(out=nw_bc[:, 0:512], in_=pb[2][:]),
      reads=(pbt[2],), writes=("nw_bc",))
    A("vector", lambda e: e.tensor_copy(out=nw_bc[:, 512:1024], in_=pb[3][:]),
      reads=(pbt[3],), writes=("nw_bc",))

    for t in range(NTILE if "C" in phases else 0):
        A("sync", (lambda e, t=t: e.dma_start(
            out=gtsT_c[:], in_=gts_scr[:, t * 512:(t + 1) * 512])),
          writes=("gtsT_c",), slot="gtg")
        for cc_i in range(4):
            c = t * 4 + cc_i
            lb = c % 2
            A("sync", (lambda e, c=c, lb=lb: e.dma_start(out=o_ld[lb][:], in_=o_scr[c])),
              writes=("o_ld%d" % lb,), slot="old%d" % lb)
            A("sync", (lambda e, c=c, lb=lb: e.dma_start(out=qh_ld[lb][:], in_=qh_scr[c])),
              writes=("qh_ld%d" % lb,), slot="qld%d" % lb)
            A("sync", (lambda e, c=c, lb=lb: e.dma_start(
                out=xt_c[lb][:], in_=dx_tm[c * 128:(c + 1) * 128, :])),
              writes=("xt_c%d" % lb,), slot="xtl%d" % lb)
            A("pool", (lambda e, c=c, lb=lb: e.indirect_dma_start(
                out=ga1[lb][:], out_offset=None, in_=o_cat,
                in_offset=bass.IndirectOffsetOnAxis(loc_all[:, c:c + 1], 0))),
              reads=("loc_all", "o_cat"), writes=("ga1_%d" % lb,), slot="ug1%d" % lb)
            A("pool", (lambda e, c=c, lb=lb: e.indirect_dma_start(
                out=ga2[lb][:], out_offset=None, in_=o_cat,
                in_offset=bass.IndirectOffsetOnAxis(loc_all[:, 32 + c:33 + c], 0))),
              reads=("loc_all", "o_cat"), writes=("ga2_%d" % lb,), slot="ug2%d" % lb)
            for h in range(16 if "corr" in CFLAGS else 0):
                obank = h // 8
                ocol = (h % 8) * 64
                mm(pb[obank][:, ocol:ocol + 64],
                   qh_ld[lb][:, h * 128:(h + 1) * 128],
                   Scor_b[:, h * 64:(h + 1) * 64],
                   True, True, ("qh_ld%d" % lb, "Scor_b"), pbt[obank], "corr")
            if "corr" in CFLAGS:
                A("vector", (lambda e, lb=lb: e.tensor_tensor(
                    out=of_sb[:, 0:512], in0=o_ld[lb][:, 0:512], in1=pb[0][:, 0:512],
                    op=ALU.add)),
                  reads=("o_ld%d" % lb, pbt[0]), writes=("of",))
                A("vector", (lambda e, lb=lb: e.tensor_tensor(
                    out=of_sb[:, 512:1024], in0=o_ld[lb][:, 512:1024], in1=pb[1][:, 0:512],
                    op=ALU.add)),
                  reads=("o_ld%d" % lb, pbt[1]), writes=("of",))
            else:
                A("vector", (lambda e, lb=lb: e.tensor_copy(out=of_sb[:, 0:1024],
                                                            in_=o_ld[lb][:, 0:1024])),
                  reads=("o_ld%d" % lb,), writes=("of",))
            if "rms" not in CFLAGS:
                A("vector", lambda e: e.tensor_copy(out=on_f[:], in_=of_sb[:]),
                  reads=("of",), writes=("on_f",))
            if "rms" not in CFLAGS:
                pass
            _n_rms = 1 if "rms" in CFLAGS else 0
            for _ in range(_n_rms):
                A("scalar", lambda e: e.activation(out=sqc[:], in_=of_sb[:], func=AF.Square),
                  reads=("of",), writes=("sqc",))
            for _ in range(_n_rms):
                A("vector", lambda e: e.reduce_sum(out=ssum[:], in_=sqc[:], axis=AX.X),
                  reads=("sqc",), writes=("ssum",))
            for _ in range(_n_rms):
                A("vector", lambda e: e.tensor_scalar(out=rstd[:], in0=ssum[:], scalar1=1.0 / 1024.0,
                                                      scalar2=1e-6, op0=ALU.mult, op1=ALU.add),
                  reads=("ssum",), writes=("rstd",))
            for _ in range(_n_rms):
                A("scalar", lambda e: e.activation(out=rstd[:], in_=rstd[:], func=AF.Sqrt),
                  reads=("rstd",), writes=("rstd",))
            for _ in range(_n_rms):
                A("vector", lambda e: e.reciprocal(out=rstd[:], in_=rstd[:]),
                  reads=("rstd",), writes=("rstd",))
            for _ in range(_n_rms):
                A("vector", lambda e: e.tensor_scalar_mul(on1[:], of_sb[:], rstd[:, 0:1]),
                  reads=("of", "rstd"), writes=("on1",))
            for _ in range(_n_rms):
                A("vector", lambda e: e.tensor_tensor(out=on_f[:], in0=on1[:], in1=nw_bc[:],
                                                      op=ALU.mult),
                  reads=("on1", "nw_bc"), writes=("on_f",))
            for blk in range(8):
                bank = 4 + (blk % 2)
                A("tensor", (lambda e, blk=blk, bank=bank: e.transpose(
                    pb[bank][:, 0:128], on_f[:, blk * 128:(blk + 1) * 128], ident[:])),
                  reads=("on_f", "ident"), writes=(pbt[bank],))
                A("vector", (lambda e, blk=blk, bank=bank, cc_i=cc_i: e.tensor_copy(
                    out=on_fm[:, blk * 512 + cc_i * 128: blk * 512 + cc_i * 128 + 128],
                    in_=pb[bank][:, 0:128])),
                  reads=(pbt[bank],), writes=("on_fm",))
            for half in range(2):
                sbank = 2 + half
                for k in range(8):
                    mm(pb[sbank][:],
                       on_fm[:, k * 512 + cc_i * 128: k * 512 + cc_i * 128 + 128],
                       wo_b[:, k * 1024 + half * 512: k * 1024 + (half + 1) * 512],
                       k == 0, k == 7, ("on_fm", "wo_b"), pbt[sbank], "sproj")
            for half in range(2):
                mm(pb[half][:], gtsT_c[:, cc_i * 128:(cc_i + 1) * 128],
                   b2_sb[:, half * 512:(half + 1) * 512], True, True,
                   ("gtsT_c", "b2_sb"), pbt[half], "b2mix")
            A("vector", (lambda e, lb=lb, c=c: e.tensor_scalar_mul(
                yt[:], ga1[lb][:], g12_all[:, c:c + 1])),
              reads=("ga1_%d" % lb, "g12_all"), writes=("yt",))
            A("vector", (lambda e, lb=lb, c=c: e.tensor_scalar_mul(
                yt2[:], ga2[lb][:], g12_all[:, 32 + c:33 + c])),
              reads=("ga2_%d" % lb, "g12_all"), writes=("yt2",))
            if dbg:
                A("pool", (lambda e, c=c: e.dma_start(out=dbg_yt1[c], in_=yt[:])),
                  reads=("yt",), slot="dbgy1")
            A("vector", lambda e: e.tensor_tensor(out=yt[:], in0=yt[:], in1=yt2[:],
                                                  op=ALU.add),
              reads=("yt", "yt2"), writes=("yt",))
            A("vector", lambda e: e.tensor_tensor(out=yt[:, 0:512], in0=yt[:, 0:512],
                                                  in1=pb[0][:], op=ALU.add),
              reads=("yt", pbt[0]), writes=("yt",))
            A("vector", lambda e: e.tensor_tensor(out=yt[:, 512:1024], in0=yt[:, 512:1024],
                                                  in1=pb[1][:], op=ALU.add),
              reads=("yt", pbt[1]), writes=("yt",))
            A("scalar", lambda e: e.activation(out=gfin[:], in_=yt[:], func=AF.Silu),
              reads=("yt",), writes=("gfin",))
            if dbg:
                A("pool", (lambda e, c=c: e.dma_start(out=dbg_g[c], in_=gfin[:])),
                  reads=("gfin",), slot="dbgg")
                A("pool", (lambda e, c=c, lb=lb: e.dma_start(out=dbg_ga1[c], in_=ga1[lb][:])),
                  reads=("ga1_%d" % lb,), slot="dbgga")
                A("pool", (lambda e, c=c: e.dma_start(out=dbg_yt[c], in_=yt[:])),
                  reads=("yt",), slot="dbgyt")
            for half in range(2):
                sbank = 2 + half
                A("vector", (lambda e, half=half, sbank=sbank: e.tensor_tensor(
                    out=tmpc[:], in0=pb[sbank][:],
                    in1=gfin[:, half * 512:(half + 1) * 512], op=ALU.mult)),
                  reads=(pbt[sbank], "gfin"), writes=("tmpc",))
                A("vector", (lambda e, half=half, lb=lb: e.tensor_tensor(
                    out=ostage_tm[lb][:, half * 512:(half + 1) * 512], in0=tmpc[:],
                    in1=xt_c[lb][:, half * 512:(half + 1) * 512], op=ALU.add)),
                  reads=("tmpc", "xt_c%d" % lb), writes=("ostage_tm%d" % lb,))
            A("pool", (lambda e, c=c, lb=lb: e.dma_start(
                out=dout[c * 128:(c + 1) * 128, :], in_=ostage_tm[lb][:])),
              reads=("ostage_tm%d" % lb,), slot="outs%d" % lb)


    ir.barrier()

    ir.resolve()
    chain, slots = {}, {}
    with contextlib.ExitStack() as semstack:
        for e in ENGINES:
            chain[e] = semstack.enter_context(nc.semaphore("c_" + e))
        for s in sorted(ir.slot_count):
            slots[s] = semstack.enter_context(nc.semaphore("s_" + s))
        with nc.Block() as block:
            @block.tensor
            def _(eng):
                ir.emit_one("tensor", eng, chain, slots)

            @block.vector
            def _(eng):
                ir.emit_one("vector", eng, chain, slots)

            @block.scalar
            def _(eng):
                ir.emit_one("scalar", eng, chain, slots)

            @block.sync
            def _(eng):
                ir.emit_one("sync", eng, chain, slots)

            @block.gpsimd
            def _(eng):
                ir.emit_one("pool", eng, chain, slots)
    stack.close()
    return nc


def kernel(L0_attn_x, L0_attn_Wq, L0_attn_Wk, L0_attn_Wv, L0_attn_Wo,
           L0_attn_Wf, L0_attn_delta, L0_attn_norm_weight, L0_ffn_Wg,
           L0_ffn_W1, L0_ffn_b1, L0_ffn_W2, L0_ffn_b2):
    f32 = np.float32
    x = np.asarray(L0_attn_x, f32)
    wqT = np.ascontiguousarray(np.asarray(L0_attn_Wq, f32).T)
    wkT = np.ascontiguousarray(np.asarray(L0_attn_Wk, f32).T)
    wvT = np.ascontiguousarray(np.asarray(L0_attn_Wv, f32).T)
    woT = np.ascontiguousarray(np.asarray(L0_attn_Wo, f32).T)
    wfT = np.ascontiguousarray(np.asarray(L0_attn_Wf, f32).T)
    wgT = np.ascontiguousarray(np.asarray(L0_ffn_Wg, f32).T)
    w1 = np.ascontiguousarray(np.asarray(L0_ffn_W1, f32))
    w2 = np.ascontiguousarray(np.asarray(L0_ffn_W2, f32))
    b1p = np.ascontiguousarray(
        np.asarray(L0_ffn_b1, f32).reshape(8, 32, 128).transpose(2, 0, 1).reshape(128, 256))
    b2 = np.ascontiguousarray(np.asarray(L0_ffn_b2, f32))
    delta = np.ascontiguousarray(np.asarray(L0_attn_delta, f32).reshape(1, 16))
    nw = np.ascontiguousarray(np.asarray(L0_attn_norm_weight, f32).reshape(1, 1024))
    triu = np.ascontiguousarray(np.triu(np.ones((128, 128), f32)))
    trius = np.ascontiguousarray(np.triu(np.ones((128, 128), f32), 1))
    ident = np.eye(128, dtype=f32)
    iota128 = np.arange(128, dtype=f32).reshape(128, 1)
    ecb = np.broadcast_to((np.arange(8, dtype=f32) * 1536.0)[None, :], (128, 8)).copy()

    nc = build_program()
    in_maps = []
    for c in range(8):
        b, half = c // 2, c % 2
        xc = np.ascontiguousarray(x[b, half * 4096:(half + 1) * 4096, :].T)
        mS = np.full((128, 1), 1.0 if half == 1 else 0.0, f32)
        mC = np.full((128, 1), 1.0 if half == 0 else 0.0, f32)
        xc_tm = np.ascontiguousarray(x[b, half * 4096:(half + 1) * 4096, :])
        in_maps.append(dict(
            x=xc, x_tm=xc_tm, wqT=wqT, wkT=wkT, wvT=wvT, woT=woT, wfT=wfT,
            wgT=wgT, w1=w1, w2=w2, b1p=b1p, b2=b2, delta=delta, nw=nw,
            triu=triu, trius=trius, ident=ident, iota128=iota128, ecb=ecb,
            maskS=mS, maskC=mC))
    res = run_bass_kernel_spmd(nc, in_maps, core_ids=list(range(8)))
    out = np.empty((4, 8192, 1024), f32)
    for c in range(8):
        b, half = c // 2, c % 2
        out[b, half * 4096:(half + 1) * 4096, :] = res.results[c]["out"]
    return out
